# version 03
# baseline (speedup 1.0000x reference)
import numpy as np
import concourse.bass as bass
import concourse.mybir as mybir
import bass_rust
from concourse.bass_utils import run_bass_kernel_spmd

F32 = mybir.dt.float32
BF16 = mybir.dt.bfloat16
F16 = mybir.dt.float16
U8 = mybir.dt.uint8
AF = mybir.ActivationFunctionType
OP = mybir.AluOpType
AX = mybir.AxisListType

P = 128
T = 2048
NT = 4
TC = 512
DM, DC = 1024, 8
DI, DIC = 2048, 16
DS = 16
FF = 4096
IH, IHC = 2816, 22
NC8 = 8
RC = 12582912.0
EPS = 1e-5

STOP_AFTER = None
SKIP = set()


def build(stop_after=None, skip=()):
    nc = bass.Bass("TRN2", target_bir_lowering=False)

    ein = lambda n, s, d=F32: nc.dram_tensor(n, s, d, kind="ExternalInput").ap()
    x_h = ein("x", [T, DM])
    wq_h, wk_h = ein("wq", [DM, DM]), ein("wk", [DM, DM])
    wv_h, wo_h = ein("wv", [DM, DM]), ein("wo", [DM, DM])
    bq_h, bk_h = ein("bq", [DM]), ein("bk", [DM])
    bv_h, bo_h = ein("bv", [DM]), ein("bo", [DM])
    k0_h, k1_h, k2_h = ein("k0", [2 * DM, FF], U8), ein("k1", [FF, FF], U8), ein("k2", [FF, DM], U8)
    sc8_h = ein("sc8", [8])
    win_h, bin_h = ein("win", [DM, 2 * DI]), ein("bin", [2 * DI])
    cw_h, cb_h = ein("cw", [DI, 4]), ein("cb", [DI])
    wx_h, bx_h = ein("wx", [DI, DS]), ein("bx", [DS])
    wdt_h, bdt_h = ein("wdt", [DI, DI]), ein("bdt", [DI])
    alog_h, dvec_h = ein("alog", [DI, DS]), ein("dvec", [DI])
    wout_h, bout_h = ein("wout", [DI, DM]), ein("bout", [DM])
    wg_h, wu_h, wd_h = ein("wg", [IH, DM]), ein("wu", [IH, DM]), ein("wd", [DM, IH])
    msk_h = ein("msk", [16])

    out_h = nc.dram_tensor("y", [T, DM], F32, kind="ExternalOutput").ap()
    dbg = {}
    if stop_after == "ph4":
        dbg["cols"] = nc.dram_tensor("dbg_cols", [P, 33], F32, kind="ExternalOutput").ap()
    if stop_after is not None:
        dbg["res"] = nc.dram_tensor("dbg_res", [P, DC, T], BF16, kind="ExternalOutput").ap()
        dbg["xn"] = nc.dram_tensor("dbg_xn", [P, DC, T], BF16, kind="ExternalOutput").ap()
        dbg["smalls"] = nc.dram_tensor("dbg_smalls", [P, 2048], F32, kind="ExternalOutput").ap()
    if stop_after in ("ph5", "ph6", "ph7"):
        dbg["s1"] = nc.dram_tensor("dbg_s1", [P, DIC, T], BF16, kind="ExternalOutput").ap()
        dbg["s2"] = nc.dram_tensor("dbg_s2", [P, DIC, T], BF16, kind="ExternalOutput").ap()
        dbg["s5"] = nc.dram_tensor("dbg_s5", [P, DIC, T], BF16, kind="ExternalOutput").ap()
        dbg["s3"] = nc.dram_tensor("dbg_s3", [P, DIC, T], F16, kind="ExternalOutput").ap()
        dbg["s4"] = nc.dram_tensor("dbg_s4", [P, DIC, T], F16, kind="ExternalOutput").ap()
    if stop_after == "ph7":
        dbg["s6"] = nc.dram_tensor("dbg_s6", [P, IHC, T], F32, kind="ExternalOutput").ap()

    scr = lambda n, s, d: nc.dram_tensor(n, s, d).ap()
    kv_in = scr("kv_in", [P, DC, DM + 1], F32)
    kv_out = scr("kv_out", [P, DC, DM + 1], F32)
    ag1_in = scr("ag1_in", [P, 52], F32)
    ag1_out = scr("ag1_out", [NC8, P, 52], F32)
    ag2_in = scr("ag2_in", [P, 516], F32)
    ag2_out = scr("ag2_out", [NC8, P, 516], F32)
    xn1_sp = scr("xn1_sp", [P, DC, T], BF16)
    xhp_sp = scr("xhp_sp", [P, DIC, T], BF16)
    xh_sp = scr("xh_sp", [P, DIC, T], BF16)
    dl_sp = scr("dl_sp", [P, DIC, T], F16)
    du_sp = scr("du_sp", [P, DIC, T], BF16)
    y_sp = scr("y_sp", [P, DIC, T], F16)
    dc_sp = scr("dc_sp", [P, DIC, T], F16)
    h_sp = scr("h_sp", [P, IHC, T], F32)
    wgq_sp = scr("wgq_sp", [IH, DM], BF16)
    wuq_sp = scr("wuq_sp", [IH, DM], BF16)
    wdq_sp = scr("wdq_sp", [DM, IH], BF16)

    from contextlib import ExitStack
    es = ExitStack()
    SEMN = ["st", "sv", "ss", "sg", "dg", "dy", "da", "cc", "dg0", "dg1"]
    SEM = {n: es.enter_context(nc.semaphore(n)) for n in SEMN}
    CNT = {n: 0 for n in SEMN}
    blk = es.enter_context(nc.Block())
    OWN = dict(t="st", v="sv", s="ss", g="sg", y=None)
    QS = dict(g="dg", y="dy", s="da")
    streams = {e: [] for e in "tvsgy"}

    def W(e, *reqs):
        reqs = [(s, v) for (s, v) in reqs if v and v > 0]
        if reqs:
            streams[e].append(("w", reqs))

    def E(e, fn, sem=None, k=1):
        val = None
        if sem is not None:
            CNT[sem] += k
            val = CNT[sem]
        streams[e].append(("e", fn, sem, k))
        return val

    def DMA(e, out, in_, q=None, **kw):
        return E(e, lambda h, out=out, in_=in_, kw=kw: h.dma_start(out=out, in_=in_, **kw),
                 sem=(q or QS[e]), k=16)

    def bar(*engs):
        snap = dict(CNT)
        for e in engs:
            W(e, *[(n, v) for n, v in snap.items() if n != OWN[e]])

    sb = lambda n, s, d: es.enter_context(nc.sbuf_tensor(n, s, d))
    res = sb("res", [P, DC, T], BF16)
    xn = sb("xn", [P, DC, T], BF16)
    AB = sb("AB", [P, 32768], BF16)
    A16 = sb("A16", [P, 16384], F16)
    AFr = sb("AFr", [P, 6144], F32)

    eye_b = sb("eye_b", [P, P], BF16)
    ones_b = sb("ones_b", [P, P], BF16)
    onesr_b = sb("onesr_b", [1, TC], BF16)
    bq_c = sb("bq_c", [P, DC], F32)
    rowA = sb("rowA", [1, DM], BF16)
    rowB = sb("rowB", [1, DM], BF16)
    binxh_c = sb("binxh_c", [P, DIC], F32)
    binz_c = sb("binz_c", [P, DIC], F32)
    cw_c = sb("cw_c", [P, DIC, 4], F32)
    cb_c = sb("cb_c", [P, DIC], F32)
    bdt_c = sb("bdt_c", [P, DIC], F32)
    dvec_c = sb("dvec_c", [P, DIC], F32)
    bx_c = sb("bx_c", [DS, 1], F32)
    wx_sb = sb("wx_sb", [P, DIC, DS], BF16)
    A_sl = sb("A_sl", [P, 256], F32)
    sc_row = sb("sc_row", [1, 8], F32)
    msk_row = sb("msk_row", [1, 16], F32)
    scb = sb("scb", [P, 16], F32)
    msk_c = sb("msk_c", [P, 16], F32)
    omsk_c = sb("omsk_c", [P, 16], F32)
    gam_sb = sb("gam_sb", [P, 256], F32)
    hend_sb = sb("hend_sb", [P, 256], BF16)
    hin_sb = sb("hin_sb", [P, 256], F32)
    hinq1 = sb("hinq1", [P, 256], F32)
    dpre_sb = sb("dpre_sb", [P, 4, 256], F16)
    dtotT = sb("dtotT", [P, 256], F32)
    ksum_b = sb("ksum_b", [P, DC], BF16)
    B_sb = sb("B_sb", [DS, T], BF16)
    sg_c = sb("sg_c", [P, IHC], F32)
    su_c = sb("su_c", [P, IHC], F32)
    sd_c = sb("sd_c", [P, DC], F32)
    gsc_c = sb("gsc_c", [P, IHC], F32)
    usc_c = sb("usc_c", [P, IHC], F32)
    dsc_c = sb("dsc_c", [P, DC], F32)
    tail_sb = sb("tail_sb", [P, DIC, 3], BF16)
    invz = sb("invz", [1, TC], BF16)
    xm_col = sb("xm_col", [P, 4], F32)
    xden = sb("xden", [P, 1], F32)

    ps = [es.enter_context(nc.psum_tensor(f"ps{i}", [P, TC], F32)) for i in range(6)]
    pbs = [es.enter_context(nc.psum_tensor(f"pb{i}", [P, 1024], BF16)) for i in range(2)]

    def ab(off, n):
        assert off + n <= 32768
        return AB[:, off:off + n]

    def ab3(off, c, l):
        assert off + c * l <= 32768
        return AB[:, off:off + c * l].rearrange("p (c l) -> p c l", c=c)

    def a6(off, n):
        assert off + n <= 16384
        return A16[:, off:off + n]

    def a63(off, c, l):
        assert off + c * l <= 16384
        return A16[:, off:off + c * l].rearrange("p (c l) -> p c l", c=c)

    def af(off, n):
        assert off + n <= 6144
        return AFr[:, off:off + n]

    def af3(off, c, l):
        assert off + c * l <= 6144
        return AFr[:, off:off + c * l].rearrange("p (c l) -> p c l", c=c)

    guard = {i: None for i in range(8)}

    def tb(banks):
        reqs = [guard[b] for b in banks if guard[b] is not None]
        if reqs:
            W("t", *reqs)

    def finish():
        fin = [(n, v) for n, v in CNT.items() if v > 0]
        streams["y"].append(("w", fin))
        EM = dict(t=blk.tensor, v=blk.vector, s=blk.scalar, g=blk.gpsimd, y=blk.sync)
        for e, items in streams.items():
            def body(h, items=items):
                for it in items:
                    if it[0] == "w":
                        for sn, val in it[1]:
                            h.wait_ge(SEM[sn], val)
                    else:
                        _, fn, sem, k = it
                        ins = fn(h)
                        if sem is not None:
                            ins.then_inc(SEM[sem], k)
            EM[e](body)
        es.close()
        return nc

    def stop(name):
        if stop_after == name:
            bar("y", "v")
            def vpk(h):
                h.tensor_copy(AFr[:, 0:256], gam_sb[:, :])
                h.tensor_copy(AFr[:, 256:512], hend_sb[:, :])
                h.tensor_copy(AFr[:, 512:768], hin_sb[:, :])
                h.tensor_copy(AFr[:, 768:1024], dtotT[:, :])
                h.tensor_copy(AFr[:, 1024:1280], dpre_sb[:, 1, :])
                h.tensor_copy(AFr[:, 1280:1296], scb[:, :])
                h.tensor_copy(AFr[:, 1296:1300], xm_col[:, :])
                h.tensor_copy(AFr[:, 1300:1332], AFr[:, 2560:2592])
                h.tensor_copy(AFr[:, 1332:1333], xden[:, :])
                h.tensor_copy(AFr[:, 1333:1421], AFr[:, 4704:4792])
                return h.tensor_copy(AFr[:, 1400:1816], af3(1024, 8, 52))
            vv = E("v", vpk, "sv")
            W("y", ("sv", vv))
            DMA("y", dbg["smalls"], AFr[:, 0:2048])
            DMA("y", dbg["res"], res[:, :, :])
            DMA("y", dbg["xn"], xn[:, :, :])
            if name in ("ph5", "ph6", "ph7"):
                DMA("y", dbg["s1"], xhp_sp)
                DMA("y", dbg["s2"], xh_sp)
                DMA("y", dbg["s5"], du_sp)
                DMA("y", dbg["s3"], dl_sp if name == "ph5" else y_sp)
                DMA("y", dbg["s4"], dc_sp)
            if name == "ph7":
                DMA("y", dbg["s6"], h_sp)
            return True
        return False

    DMA("g", bq_c[:, :], bq_h.rearrange("(c p) -> p c", p=P), allow_slow_non_contiguous=True)
    DMA("g", rowA[:, :], bk_h[None, :])
    DMA("g", rowB[:, :], bv_h[None, :])
    DMA("g", binxh_c[:, :], bin_h[0:DI].rearrange("(c p) -> p c", p=P), allow_slow_non_contiguous=True)
    DMA("g", binz_c[:, :], bin_h[DI:2 * DI].rearrange("(c p) -> p c", p=P), allow_slow_non_contiguous=True)
    DMA("g", cw_c[:, :, :], cw_h.rearrange("(c p) k -> p c k", p=P))
    DMA("g", cb_c[:, :], cb_h.rearrange("(c p) -> p c", p=P), allow_slow_non_contiguous=True)
    DMA("g", bdt_c[:, :], bdt_h.rearrange("(c p) -> p c", p=P), allow_slow_non_contiguous=True)
    DMA("g", dvec_c[:, :], dvec_h.rearrange("(c p) -> p c", p=P), allow_slow_non_contiguous=True)
    DMA("g", bx_c[:, :], bx_h[:, None])
    DMA("g", wx_sb[:, :, :], wx_h.rearrange("(c p) s -> p c s", p=P))
    DMA("g", A_sl[:, :].rearrange("p (c s) -> p c s", c=DIC),
        alog_h.rearrange("(c p) s -> p c s", p=P))
    DMA("g", sc_row[:, :], sc8_h[None, :])
    dv0 = DMA("g", msk_row[:, :], msk_h[None, :])
    E("g", lambda h: h.iota(af(0, P), [[0, P]], channel_multiplier=1,
                            allow_small_or_imprecise_dtypes=True), "sg")
    E("g", lambda h: h.iota(af(P, P), [[1, P]], channel_multiplier=0,
                            allow_small_or_imprecise_dtypes=True), "sg")
    E("g", lambda h: h.memset(ones_b[:, :], 1.0), "sg")
    E("g", lambda h: h.memset(af(512, P), 1.0), "sg")
    E("g", lambda h: h.memset(scb[:, 11:12], EPS), "sg")
    E("g", lambda h: h.memset(scb[:, 12:13], 1.0), "sg")
    E("g", lambda h: h.memset(onesr_b[:, :], 1.0), "sg")
    E("g", lambda h: h.memset(hin_sb[:, :], 0.0), "sg")
    E("g", lambda h: h.memset(dpre_sb[:, :, :], 0.0), "sg")
    E("g", lambda h: h.memset(tail_sb[:, :, :], 0.0), "sg")
    bar("v")
    E("v", lambda h: h.tensor_tensor(eye_b[:, :], af(0, P), af(P, P), OP.is_equal), "sv")
    bar("t")
    E("t", lambda h: h.matmul(ps[0][:, 0:8], af(512, P)[0:1, :], sc_row[0:1, :],
                              start=True, stop=True), "st")
    tv = E("t", lambda h: h.matmul(ps[0][:, 8:24], af(512, P)[0:1, :], msk_row[0:1, :],
                                   start=True, stop=True), "st")
    W("v", ("st", tv))
    def _c0(v):
        v.tensor_copy(msk_c[:, :], ps[0][:, 8:24])
        v.tensor_scalar(omsk_c[:, :], ps[0][:, 8:24], -1.0, 1.0, OP.mult, OP.add)
        v.tensor_scalar(scb[:, 0:1], ps[0][:, 0:1], 4.0, None, OP.mult)
        v.tensor_scalar(scb[:, 1:2], ps[0][:, 1:2], 2.0, None, OP.mult)
        v.tensor_scalar(scb[:, 2:3], ps[0][:, 1:2], 1.0, None, OP.mult)
        v.tensor_scalar(scb[:, 3:4], ps[0][:, 2:3], 2.0, None, OP.mult)
        v.tensor_scalar(scb[:, 4:5], ps[0][:, 2:3], 1.0, None, OP.mult)
        return v.tensor_copy(af(256, 2), ps[0][:, 3:5])
    vv = E("v", _c0, "sv")
    guard[0] = ("sv", vv)
    W("s", ("sv", vv))
    sv = E("s", lambda h: h.activation(af(260, 2), af(256, 2), AF.Exp, scale=-1.0), "ss")
    W("v", ("ss", sv))
    def _c1(v):
        v.tensor_scalar(af(264, 2), af(260, 2), 1.0, None, OP.add)
        v.tensor_copy(af(280, 8), af(256, 8))
        v.tensor_copy(af(280, 8), af(256, 8))
        return v.reciprocal(scb[:, 5:7], af(264, 2))
    E("v", _c1, "sv")
    bar("s")
    sv = E("s", lambda h: h.activation(A_sl[:, :], A_sl[:, :], AF.Exp), "ss")
    W("v", ("ss", sv))
    E("v", lambda h: h.tensor_scalar(A_sl[:, :], A_sl[:, :], -1.0, None, OP.mult), "sv")

    last_tp = 0
    for xh2 in range(2):
        if last_tp:
            W("g", ("st", last_tp))
        dvx = DMA("g", ab3(0, 8, 1024),
                  x_h[xh2 * 1024:(xh2 + 1) * 1024, :].rearrange("(tc p) d -> p tc d", p=P))
        W("t", ("dg", dvx))
        for dc in range(8):
            for tg in range(2):
                bk = 6 + (tg % 2)
                tb([bk])
                def tf(h, dc=dc, tg=tg, bk=bk):
                    last = None
                    for j in range(4):
                        last = h.transpose(pbs[bk - 6][:, j * P:(j + 1) * P],
                                           ab3(0, 8, 1024)[:, tg * 4 + j, dc * P:(dc + 1) * P],
                                           eye_b[:, :])
                    return last
                tv = E("t", tf, "st")
                last_tp = tv
                W("v", ("st", tv))
                vv = E("v", lambda h, dc=dc, tg=tg, bk=bk, xh2=xh2:
                       h.tensor_copy(res[:, dc, xh2 * 1024 + tg * 512:xh2 * 1024 + (tg + 1) * 512],
                                     pbs[bk - 6][:, 0:512]), "sv")
                guard[bk] = ("sv", vv)

    MEAN, M2, VAR, INV = 512, 1024, 1536, 2048

    def layernorm(dst, absmax_to=None):
        for tc in range(NT):
            sl = slice(tc * TC, (tc + 1) * TC)
            W("s", ("sv", CNT["sv"]))
            def sf(h, sl=sl):
                last = None
                for c in range(8):
                    last = h.activation(ab3(8192, 8, TC)[:, c, :], res[:, c, sl], AF.Square)
                return last
            sv = E("s", sf, "ss")
            tb([0, 1])
            W("t", ("ss", sv), ("sv", CNT["sv"]))
            def tf(h, sl=sl):
                for c in range(8):
                    h.matmul(ps[0][:, :], ones_b[:, :], res[:, c, sl],
                             start=(c == 0), stop=(c == 7))
                last = None
                for c in range(8):
                    last = h.matmul(ps[1][:, :], ones_b[:, :], ab3(8192, 8, TC)[:, c, :],
                                    start=(c == 0), stop=(c == 7))
                return last
            tv = E("t", tf, "st")
            W("v", ("st", tv))
            def vf(h):
                h.tensor_scalar(af(MEAN, TC), ps[0][:, :], 1.0 / DM, None, OP.mult)
                h.tensor_tensor(af(M2, TC), af(MEAN, TC), af(MEAN, TC), OP.mult)
                return h.scalar_tensor_tensor(af(VAR, TC), ps[1][:, :], 1.0 / DM,
                                              af(M2, TC), OP.mult, OP.subtract)
            vv = E("v", vf, "sv")
            W("s", ("sv", vv))
            sv2 = E("s", lambda h: h.activation(af(VAR, TC), af(VAR, TC), AF.Sqrt,
                                                bias=scb[:, 11:12]), "ss")
            W("v", ("ss", sv2))
            def vf2(h, sl=sl, dst=dst):
                h.reciprocal(af(INV, TC), af(VAR, TC))
                last = None
                for c in range(8):
                    h.tensor_tensor(af(M2, TC), res[:, c, sl], af(MEAN, TC), OP.subtract)
                    last = h.tensor_tensor(dst[:, c, sl], af(M2, TC), af(INV, TC), OP.mult)
                return last
            vv2 = E("v", vf2, "sv")
            guard[0] = ("sv", vv2)
            guard[1] = ("sv", vv2)
            if absmax_to is not None:
                def vf3(h, sl=sl, dst=dst, tc=tc):
                    last = None
                    for c in range(8):
                        last = h.tensor_reduce(af(2560 + tc * 8 + c, 1), dst[:, c, sl],
                                               AX.X, OP.max, apply_absolute_value=True)
                    return last
                E("v", vf3, "sv")

    layernorm(xn)
    if stop("ph1"):
        return finish()

    S0, S1, S2, S3 = 0, 8192, 16384, 24576

    def load_wT(src_h, dst_off):
        dv = DMA("g", ab3(S1, 8, 1024), src_h.rearrange("(rc p) d -> p rc d", p=P))
        W("t", ("dg", dv))
        for ec in range(8):
            for grp in range(2):
                bk = 6 + grp
                tb([bk])
                def tf(h, ec=ec, grp=grp, bk=bk):
                    last = None
                    for j in range(4):
                        oc = grp * 4 + j
                        last = h.transpose(pbs[bk - 6][:, j * P:(j + 1) * P],
                                           ab3(S1, 8, 1024)[:, oc, ec * P:(ec + 1) * P],
                                           eye_b[:, :])
                    return last
                tv = E("t", tf, "st")
                W("s", ("st", tv))
                sv = E("s", lambda h, ec=ec, grp=grp, bk=bk, dst_off=dst_off:
                       h.copy(ab3(dst_off, 8, 1024)[:, ec, grp * 512:(grp + 1) * 512],
                              pbs[bk - 6][:, 0:512]), "ss")
                guard[bk] = ("ss", sv)

    bar("g")
    W("g", ("sv", CNT["sv"]), ("ss", CNT["ss"]))
    load_wT(wk_h, S2)
    bar("g")
    W("g", ("ss", CNT["ss"]))
    load_wT(wv_h, S3)

    bar("t")
    for hh in (() if "ph2" in skip else range(2)):
        for mat_off, wt_off, brow, is_k in ((S0, S2, rowA, True), (S1, S3, rowB, False)):
            mat = ab3(mat_off, 8, 1024)
            for tsub in range(8):
                for dh in range(2):
                    bk = (tsub * 2 + dh) % 4
                    tb([bk])
                    def tf(h, hh=hh, tsub=tsub, dh=dh, bk=bk, wt_off=wt_off, brow=brow):
                        t0 = hh * 1024 + tsub * P
                        for kc in range(8):
                            h.matmul(ps[bk][:, :], xn[:, kc, t0:t0 + P],
                                     ab3(wt_off, 8, 1024)[:, kc, dh * 512:(dh + 1) * 512],
                                     start=(kc == 0), stop=False)
                        return h.matmul(ps[bk][:, :], onesr_b[0:1, 0:P],
                                        brow[0:1, dh * 512:(dh + 1) * 512],
                                        start=False, stop=True)
                    tv = E("t", tf, "st")
                    if is_k:
                        W("s", ("st", tv))
                        def sf(h, bk=bk):
                            h.activation(a6(0, TC), ps[bk][:, :], AF.Exp)
                            return h.activation(a6(512, TC), ps[bk][:, :], AF.Relu)
                        sv = E("s", sf, "ss")
                        W("v", ("ss", sv))
                        vv = E("v", lambda h, mat=mat, tsub=tsub, dh=dh:
                               h.scalar_tensor_tensor(mat[:, tsub, dh * 512:(dh + 1) * 512],
                                                      a6(0, TC), 1.0, a6(512, TC),
                                                      OP.min, OP.add), "sv")
                        guard[bk] = ("sv", vv)
                    else:
                        W("s", ("st", tv))
                        sv = E("s", lambda h, mat=mat, tsub=tsub, dh=dh, bk=bk:
                               h.copy(mat[:, tsub, dh * 512:(dh + 1) * 512], ps[bk][:, :]),
                               "ss")
                        guard[bk] = ("ss", sv)
        kt, vt = ab3(S0, 8, 1024), ab3(S1, 8, 1024)
        W("t", ("ss", CNT["ss"]), ("sv", CNT["sv"]))
        last_acc = 0
        for eh in (() if "ph2kv" in skip else range(2)):
            for dcg in ((0, 1, 2, 3, 4, 5), (6, 7)):
                tb(list(range(len(dcg))))
                def tf(h, eh=eh, dcg=dcg):
                    last = None
                    for i, dc in enumerate(dcg):
                        for tsub in range(8):
                            last = h.matmul(ps[i][:, :], kt[:, tsub, dc * P:(dc + 1) * P],
                                            vt[:, tsub, eh * 512:(eh + 1) * 512],
                                            start=(tsub == 0), stop=(tsub == 7))
                    return last
                tv = E("t", tf, "st")
                if "ph2ke" in skip:
                    guard_upd = ("st", tv)
                    for i in range(len(dcg)):
                        guard[i] = guard_upd
                    continue
                W("s", ("st", tv), ("dg", last_acc))
                def vf(h, dcg=dcg):
                    last = None
                    for i, dc in enumerate(dcg):
                        last = h.copy(af3(0, 6, TC)[:, i, :], ps[i][:, :])
                    return last
                vv = E("s", vf, "ss")
                for i in range(len(dcg)):
                    guard[i] = ("ss", vv)
                W("g", ("ss", vv))
                if "ph2kd" in skip:
                    continue
                kw = {} if (hh == 0 or "noacc" in skip) else dict(accum_op=OP.add)
                last_acc = DMA("g", kv_in[:, dcg[0]:dcg[-1] + 1, eh * 512:(eh + 1) * 512],
                               af3(0, len(dcg), TC), **kw)
        if "ph2ks" in skip:
            continue
        tb([5])
        def tfk(h):
            last = None
            for dc in range(8):
                for tsub in range(8):
                    last = h.matmul(ps[5][:, dc:dc + 1], kt[:, tsub, dc * P:(dc + 1) * P],
                                    ones_b[:, 0:1], start=(tsub == 0), stop=(tsub == 7))
            return last
        tv = E("t", tfk, "st")
        W("v", ("st", tv), ("dg", last_acc))
        vv = E("v", lambda h: h.tensor_copy(af(3584, 8), ps[5][:, 0:8]), "sv")
        guard[5] = ("sv", vv)
        W("g", ("sv", vv))
        kw = {} if (hh == 0 or "noacc" in skip) else dict(accum_op=OP.add)
        last_acc = DMA("g", kv_in[:, :, 1024], af(3584, 8), allow_slow_non_contiguous=True, **kw)

    PH2ON = "ph2" not in skip
    if PH2ON:
        if "ph2ar" not in skip:
            cv = E("g", lambda h: h.collective_compute(
                "AllReduce", OP.add, replica_groups=[list(range(NC8))],
                ins=[kv_in.opt()], outs=[kv_out.opt()]), "cc")

        if "ph2w" not in skip:
            W("g", ("sv", CNT["sv"]), ("st", CNT["st"]))
            load_wT(wq_h, S2)
            bar("g")
            W("g", ("ss", CNT["ss"]))
            load_wT(wo_h, S3)
        if "ph2ar" not in skip:
            W("g", ("cc", cv), ("st", CNT["st"]), ("ss", CNT["ss"]))
            DMA("g", ab3(S1, 8, 1024), kv_out[:, :, 0:1024])
            dvkv = DMA("g", ksum_b[:, :], kv_out[:, :, 1024], allow_slow_non_contiguous=True)
        else:
            dvkv = 0
        kvb = ab3(S1, 8, 1024)

        W("g", ("st", CNT["st"]))
        dvbo = DMA("g", rowA[:, :], bo_h[None, :])
        W("t", ("dg", dvkv), ("dg", dvbo), ("ss", CNT["ss"]))
        qt = ab3(S0, 8, TC)
        ctx = ab3(S0 + 4096, 8, TC)
        for tc in (() if "ph2c" in skip else range(NT)):
            sl = slice(tc * TC, (tc + 1) * TC)
            for qc in range(8):
                bk = qc % 4
                tb([bk])
                def tf(h, qc=qc, bk=bk, sl=sl):
                    last = None
                    for kc in range(8):
                        last = h.matmul(ps[bk][:, :], ab3(S2, 8, 1024)[:, kc, qc * P:(qc + 1) * P],
                                        xn[:, kc, sl], start=(kc == 0), stop=(kc == 7))
                    return last
                tv = E("t", tf, "st")
                W("s", ("st", tv))
                def sf(h, qc=qc, bk=bk):
                    h.activation(a6(0, TC), ps[bk][:, :], AF.Exp, bias=bq_c[:, qc:qc + 1])
                    return h.activation(a6(512, TC), ps[bk][:, :], AF.Relu, bias=bq_c[:, qc:qc + 1])
                sv = E("s", sf, "ss")
                W("v", ("ss", sv))
                vv = E("v", lambda h, qc=qc:
                       h.scalar_tensor_tensor(qt[:, qc, :], a6(0, TC), 1.0, a6(512, TC),
                                              OP.min, OP.add), "sv")
                guard[bk] = ("sv", vv)
            if "ph2z" not in skip:
                tb([4, 5])
                W("t", ("sv", CNT["sv"]))
                tv = E("t", lambda h: [h.matmul(ps[4][0:1, :], ksum_b[:, kc:kc + 1], qt[:, kc, :],
                                                start=(kc == 0), stop=(kc == 7))
                                       for kc in range(8)][-1], "st")
                W("v", ("st", tv))
                def vfi(h):
                    with nc.allow_low_precision(reason="invZ bf16 matches tolerance"):
                        return h.reciprocal(invz[0:1, :], ps[4][0:1, :])
                vv = E("v", vfi, "sv")
                guard[4] = ("sv", vv)
                W("t", ("sv", vv))
                tv = E("t", lambda h: h.matmul(ps[5][:, :], ones_b[0:1, :], invz[0:1, :],
                                               start=True, stop=True), "st")
                W("v", ("st", tv))
                def vfq(h):
                    last = None
                    for qc in range(8):
                        last = h.tensor_tensor(qt[:, qc, :], qt[:, qc, :], ps[5][:, :], OP.mult)
                    return last
                vv = E("v", vfq, "sv")
                guard[5] = ("sv", vv)
            for ec in range(8):
                bk = ec % 4
                tb([bk])
                W("t", ("sv", vv))
                def tf(h, ec=ec, bk=bk):
                    last = None
                    for dc in range(8):
                        last = h.matmul(ps[bk][:, :], kvb[:, dc, ec * P:(ec + 1) * P],
                                        qt[:, dc, :], start=(dc == 0), stop=(dc == 7))
                    return last
                tv = E("t", tf, "st")
                W("s", ("st", tv))
                sv = E("s", lambda h, ec=ec, bk=bk: h.copy(ctx[:, ec, :], ps[bk][:, :]), "ss")
                guard[bk] = ("ss", sv)
            for oc in range(8):
                bk = 4 + (oc % 2)
                tb([bk])
                W("t", ("ss", CNT["ss"]))
                def tf(h, oc=oc, bk=bk):
                    for ec in range(8):
                        h.matmul(ps[bk][:, :], ab3(S3, 8, 1024)[:, ec, oc * P:(oc + 1) * P],
                                 ctx[:, ec, :], start=(ec == 0), stop=False)
                    return h.matmul(ps[bk][:, :], rowA[0:1, oc * P:(oc + 1) * P],
                                    onesr_b[0:1, :], start=False, stop=True)
                tv = E("t", tf, "st")
                W("v", ("st", tv))
                vv = E("v", lambda h, oc=oc, bk=bk, sl=sl:
                       h.scalar_tensor_tensor(res[:, oc, sl], ps[bk][:, :], 1.0,
                                              res[:, oc, sl], OP.mult, OP.add), "sv")
                guard[bk] = ("sv", vv)
    if stop("ph2"):
        return finish()

    bar("t", "v", "s", "g", "y")
    g1, g2 = ab3(0, 32, TC), a63(0, 32, TC)
    k0r_p = [ab3(S2, 16, 256), ab3(S2 + 4096, 16, 256)]
    k0d = ab3(S2 + 8192, 8, 256)
    k1b_p = [ab3(S2, 32, P), ab3(S2 + 4096, 32, P)]
    k2b_p = [ab3(S2, 8, 1024), ab3(S2 + 8192, 8, 1024)]
    for q in (() if "ph3" in skip else range(NT)):
        sl = slice(q * TC, (q + 1) * TC)
        dko = {}
        W("g", ("st", CNT["st"]), ("sv", CNT["sv"]))
        for og in range(2):
            dko[og] = DMA("g", k0r_p[og % 2][:, :, :],
                          k0_h[:, og * 256:(og + 1) * 256].rearrange(
                              "(kc p) n -> p kc n", p=P), q="dg%d" % (og % 2))
        for og in range(16):
            W("v", ("dg%d" % (og % 2), dko[og]), ("st", CNT["st"]))
            def vf(h, og=og):
                last = None
                for kc in range(8):
                    last = h.tensor_tensor(k0d[:, kc, :], k0r_p[og % 2][:, kc, :],
                                           k0r_p[og % 2][:, kc + 8, :], OP.subtract)
                return last
            vv = E("v", vf, "sv")
            if og + 2 < 16:
                W("g", ("sv", vv))
                dko[og + 2] = DMA("g", k0r_p[og % 2][:, :, :],
                                  k0_h[:, (og + 2) * 256:(og + 3) * 256].rearrange(
                                      "(kc p) n -> p kc n", p=P), q="dg%d" % (og % 2))
            for oc2 in range(2):
                bk = (og * 2 + oc2) % 4
                tb([bk])
                W("t", ("sv", vv))
                def tf(h, sl=sl, oc2=oc2, bk=bk):
                    last = None
                    for kc in range(8):
                        last = h.matmul(ps[bk][:, :], k0d[:, kc, oc2 * P:(oc2 + 1) * P],
                                        xn[:, kc, sl], start=(kc == 0), stop=(kc == 7))
                    return last
                tv = E("t", tf, "st")
                W("s", ("st", tv))
                sv = E("s", lambda h, og=og, oc2=oc2, bk=bk:
                       h.activation(g1[:, og * 2 + oc2, :], ps[bk][:, :], AF.Relu,
                                    scale=scb[:, 0:1]), "ss")
                guard[bk] = ("ss", sv)
        tb([4])
        W("t", ("ss", CNT["ss"]))
        tv = E("t", lambda h: [h.matmul(ps[4][:, :], ones_b[:, :], g1[:, kc, :],
                                        start=(kc == 0), stop=(kc == 31))
                               for kc in range(32)][-1], "st")
        W("v", ("st", tv))
        vv_rs = E("v", lambda h: h.tensor_scalar(af(0, TC), ps[4][:, :], scb[:, 2:3],
                                                 None, OP.mult), "sv")
        guard[4] = ("sv", vv_rs)
        dk1 = {}
        W("g", ("st", CNT["st"]), ("sv", CNT["sv"]))
        for og in range(2):
            dk1[og] = DMA("g", k1b_p[og % 2][:, :, :],
                          k1_h[:, og * P:(og + 1) * P].rearrange(
                              "(kc p) n -> p kc n", p=P), q="dg%d" % (og % 2))
        for og in range(32):
            bk = og % 4
            tb([bk])
            W("t", ("dg%d" % (og % 2), dk1[og]))
            def tf(h, og=og, bk=bk):
                last = None
                for kc in range(32):
                    last = h.matmul(ps[bk][:, :], k1b_p[og % 2][:, kc, :],
                                    g1[:, kc, :], start=(kc == 0), stop=(kc == 31))
                return last
            tv = E("t", tf, "st")
            if og + 2 < 32:
                W("g", ("st", tv))
                dk1[og + 2] = DMA("g", k1b_p[og % 2][:, :, :],
                                  k1_h[:, (og + 2) * P:(og + 3) * P].rearrange(
                                      "(kc p) n -> p kc n", p=P), q="dg%d" % (og % 2))
            W("v", ("st", tv), ("sv", vv_rs))
            vv = E("v", lambda h, bk=bk:
                   h.scalar_tensor_tensor(af(2048, TC), ps[bk][:, :], scb[:, 1:2],
                                          af(0, TC), OP.mult, OP.subtract), "sv")
            W("s", ("sv", vv))
            sv = E("s", lambda h, og=og:
                   h.activation(g2[:, og, :], af(2048, TC), AF.Relu, scale=2.0), "ss")
            guard[bk] = ("ss", sv)
        tb([4])
        W("t", ("ss", CNT["ss"]))
        tv = E("t", lambda h: [h.matmul(ps[4][:, :], ones_b[:, :], g2[:, kc, :],
                                        start=(kc == 0), stop=(kc == 31))
                               for kc in range(32)][-1], "st")
        W("v", ("st", tv))
        vv_rs2 = E("v", lambda h: h.tensor_scalar(af(512, TC), ps[4][:, :], scb[:, 4:5],
                                                  None, OP.mult), "sv")
        guard[4] = ("sv", vv_rs2)
        dk2 = {}
        W("g", ("st", CNT["st"]))
        for kq in range(2):
            dk2[kq] = DMA("g", k2b_p[kq % 2][:, :, :],
                          k2_h[kq * 1024:(kq + 1) * 1024, :].rearrange(
                              "(kc p) n -> p kc n", p=P), q="dg%d" % (kq % 2))
        for ocg in range(2):
            for kq in range(4):
                if not (ocg == 0 and kq < 2):
                    W("g", ("st", CNT["st"]))
                    dk2[kq] = DMA("g", k2b_p[kq % 2][:, :, :],
                                  k2_h[kq * 1024:(kq + 1) * 1024, :].rearrange(
                                      "(kc p) n -> p kc n", p=P), q="dg%d" % (kq % 2))
                for oc4 in range(4):
                    oc = ocg * 4 + oc4
                    bk = [0, 1, 4, 5][oc4]
                    if kq == 0:
                        tb([bk])
                    W("t", ("dg%d" % (kq % 2), dk2[kq]))
                    def tf(h, kq=kq, oc=oc, bk=bk):
                        last = None
                        for kc in range(8):
                            last = h.matmul(ps[bk][:, :], k2b_p[kq % 2][:, kc, oc * P:(oc + 1) * P],
                                            g2[:, kq * 8 + kc, :],
                                            start=(kq == 0 and kc == 0),
                                            stop=(kq == 3 and kc == 7))
                        return last
                    tv = E("t", tf, "st")
                    if kq == 3:
                        W("v", ("st", tv), ("sv", vv_rs2))
                        def vf(h, oc=oc, bk=bk, sl=sl):
                            h.scalar_tensor_tensor(af(2048, TC), ps[bk][:, :], scb[:, 3:4],
                                                   af(512, TC), OP.mult, OP.subtract)
                            return h.tensor_tensor(res[:, oc, sl], res[:, oc, sl],
                                                   af(2048, TC), OP.add)
                        vv = E("v", vf, "sv")
                        guard[bk] = ("sv", vv)
    if stop("ph3"):
        return finish()

    bar("t", "v", "s", "g", "y")
    layernorm(xn, absmax_to=xm_col[:, 0:1])
    def vvdf(h):
        h.tensor_copy(AFr[:, 3200:3232], AFr[:, 2560:2592])
        h.tensor_copy(AFr[:, 3200:3232], AFr[:, 2560:2592])
        return h.tensor_scalar(af(5950, 32), af(2560, 32), 1.0, 0.0,
                               OP.mult, OP.max, accum_out=xden[:, :])
    vvd = E("v", vvdf, "sv")
    if stop_after == "ph4":
        def vdbg(h):
            h.tensor_copy(AFr[:, 3072:3104], af(2560, 32))
            return h.tensor_copy(AFr[:, 3104:3105], xden[:, :])
        vq = E("v", vdbg, "sv")
        W("y", ("sv", vq))
        DMA("y", dbg["cols"], AFr[:, 3072:3105])
    W("g", ("sv", CNT["sv"]))
    gv = E("g", lambda h: h.tensor_reduce(invz[0:1, 0:1], xden[:, :], AX.C, OP.max),
           "sg")
    tb([5])
    W("t", ("sg", gv))
    tv = E("t", lambda h: h.matmul(ps[5][:, 0:1], ones_b[0:1, :], invz[0:1, 0:1],
                                   start=True, stop=True), "st")
    W("v", ("st", tv))
    vv = E("v", lambda h: h.tensor_copy(xm_col[:, 0:1], ps[5][:, 0:1]), "sv")
    guard[5] = ("sv", vv)
    W("y", ("sv", CNT["sv"]))
    DMA("y", xn1_sp[:, :, :], xn[:, :, :])

    def quant_weight(src_h, nrc, L, dst_sp, sc_dst):
        last_dma = 0
        for ic in range(nrc):
            W("g", ("sv", CNT["sv"]))
            dv = DMA("g", af(0, L), src_h[ic * P:(ic + 1) * P, :])
            W("v", ("dg", dv), ("dy", last_dma))
            def vf(h, ic=ic, sc_dst=sc_dst, L=L):
                h.tensor_reduce(af(5888, 1), af(0, L), AX.X, OP.max,
                                apply_absolute_value=True)
                h.tensor_copy(af(5952, 64), af(0, 64))
                h.tensor_copy(af(5952, 64), af(0, 64))
                h.tensor_scalar(sc_dst[:, ic:ic + 1], af(5888, 1), 1.0 / 127.0,
                                None, OP.mult)
                h.reciprocal(af(5890, 1), af(5888, 1))
                h.tensor_copy(af(5952, 64), af(0, 64))
                h.tensor_copy(af(5952, 64), af(0, 64))
                h.tensor_scalar(af(3072, L), af(0, L), af(5890, 1), 127.0,
                                OP.mult, OP.mult)
                return h.tensor_scalar(ab(0, L), af(3072, L), RC, -RC,
                                       OP.add, OP.add)
            vv = E("v", vf, "sv")
            W("y", ("sv", vv))
            last_dma = DMA("y", dst_sp[ic * P:(ic + 1) * P, :], ab(0, L))
    quant_weight(wg_h, IHC, DM, wgq_sp, sg_c)
    quant_weight(wu_h, IHC, DM, wuq_sp, su_c)
    quant_weight(wd_h, DC, IH, wdq_sp, sd_c)
    if stop("ph4"):
        return finish()

    bar("t", "v", "s", "g", "y")
    dv_wi = DMA("g", ab3(0, 8, DI), win_h[:, 0:DI].rearrange("(kc p) m -> p kc m", p=P))
    W("t", ("dg", dv_wi), ("sv", CNT["sv"]))
    xhpq_a = ab3(S2, 16, 516)
    last_spill = 0
    for q in range(NT):
        sl = slice(q * TC, (q + 1) * TC)
        for m in range(DIC):
            bk = m % 4
            tb([bk])
            def tf(h, sl=sl, m=m, bk=bk):
                last = None
                for kc in range(8):
                    last = h.matmul(ps[bk][:, :], ab3(0, 8, DI)[:, kc, m * P:(m + 1) * P],
                                    xn[:, kc, sl], start=(kc == 0), stop=(kc == 7))
                return last
            tv = E("t", tf, "st")
            W("s", ("st", tv), ("dy", last_spill))
            sv = E("s", lambda h, m=m, bk=bk:
                   h.activation(xhpq_a[:, m, 3:515], ps[bk][:, :], AF.Identity,
                                bias=binxh_c[:, m:m + 1]), "ss")
            guard[bk] = ("ss", sv)
        W("y", ("ss", CNT["ss"]))
        last_spill = DMA("y", xhp_sp[:, :, sl], xhpq_a[:, :, 3:515])
    W("v", ("ss", CNT["ss"]), ("sg", CNT["sg"]))
    def vagp(h):
        h.tensor_copy(af3(0, 16, 3), xhpq_a[:, :, 512:515])
        return h.tensor_copy(af(48, 1), xm_col[:, 0:1])
    vv = E("v", vagp, "sv")
    W("y", ("sv", vv))
    dvh = DMA("y", ag1_in[:, 0:49], af(0, 49))
    W("g", ("dy", dvh))
    cv = E("g", lambda h: h.collective_compute(
        "AllGather", OP.bypass, replica_groups=[list(range(NC8))],
        ins=[ag1_in.opt()], outs=[ag1_out.opt()]), "cc")
    W("g", ("cc", cv))
    dvh2 = DMA("g", af3(1024, 8, 52), ag1_out[:, :, :].rearrange("j p f -> p j f"))
    W("v", ("dg", dvh2))
    def vsel(h):
        agt = af3(1024, 8, 52)
        h.tensor_copy(af(2100, 8), agt[:, :, 48])
        h.memset(af(2052, 48), 0.0)
        for j in range(4):
            h.scalar_tensor_tensor(af(2052, 48), agt[:, j, 0:48], msk_c[:, j:j + 1],
                                   af(2052, 48), OP.mult, OP.add)
        h.tensor_reduce(xm_col[:, 2:3], af(2100, 8), AX.X, OP.max)
        for j in range(4, 8):
            h.scalar_tensor_tensor(af(2052, 48), agt[:, j, 0:48], msk_c[:, j:j + 1],
                                   af(2052, 48), OP.mult, OP.add)
        h.tensor_scalar(af(2048, 1), scb[:, 5:6], xm_col[:, 2:3], None, OP.mult)
        h.tensor_copy(af(2108, 8), agt[:, :, 48])
        h.tensor_copy(af(2108, 8), agt[:, :, 48])
        h.reciprocal(af(2049, 1), af(2048, 1))
        h.tensor_copy(af(2108, 8), agt[:, :, 48])
        h.tensor_copy(af(2108, 8), agt[:, :, 48])
        h.tensor_scalar(scb[:, 7:8], af(2049, 1), 127.0, None, OP.mult)
        h.tensor_scalar(scb[:, 8:9], af(2048, 1), 1.0 / 127.0, None, OP.mult)
        h.tensor_copy(af(2108, 8), agt[:, :, 48])
        h.tensor_copy(af(2108, 8), agt[:, :, 48])
        h.tensor_scalar(gsc_c[:, :], sg_c[:, :], scb[:, 8:9], None, OP.mult)
        return h.tensor_scalar(usc_c[:, :], su_c[:, :], scb[:, 8:9], None, OP.mult)
    vv_sel = E("v", vsel, "sv")

    bar("t", "v", "s", "g", "y")
    xhpq = ab3(0, 16, 516)
    xhq = ab3(8256, 16, TC)
    duq = ab3(16448, 16, TC)
    taps = ab3(24640, 4, TC)
    dlq = a63(0, 16, TC)
    wdt_p = [ab3(26688, 16, P), ab3(26688 + 2048, 16, P)]
    last_tail = 0
    last_q_dmas = 0
    for q in range(NT):
        sl = slice(q * TC, (q + 1) * TC)
        W("y", ("sg", last_tail), ("ss", CNT["ss"]), ("sv", CNT["sv"]))
        dv = DMA("y", xhpq[:, :, 3:515], xhp_sp[:, :, sl])
        if q == 0:
            W("v", ("dy", dv), ("sv", vv_sel))
            vv0 = E("v", lambda h: h.tensor_copy(xhpq[:, :, 0:3], af3(2052, 16, 3)), "sv")
        else:
            W("v", ("dy", dv), ("sg", last_tail))
            vv0 = E("v", lambda h: h.tensor_copy(xhpq[:, :, 0:3], tail_sb[:, :, :]), "sv")
        W("g", ("dy", dv), ("sv", vv0))
        last_tail = E("g", lambda h: h.tensor_copy(tail_sb[:, :, :], xhpq[:, :, 512:515]),
                      "sg")
        for db in range(DIC):
            def sf(h, db=db):
                last = None
                for k in range(4):
                    last = h.activation(taps[:, k, :], xhpq[:, db, k:k + TC], AF.Identity,
                                        scale=cw_c[:, db, k:k + 1])
                return last
            W("s", ("sv", vv0 if db == 0 else CNT["sv"]))
            sv = E("s", sf, "ss")
            W("v", ("ss", sv))
            def vf(h):
                h.tensor_tensor(taps[:, 0, :], taps[:, 0, :], taps[:, 1, :], OP.add)
                h.tensor_tensor(taps[:, 2, :], taps[:, 2, :], taps[:, 3, :], OP.add)
                return h.tensor_tensor(taps[:, 0, :], taps[:, 0, :], taps[:, 2, :], OP.add)
            vv = E("v", vf, "sv")
            W("s", ("sv", vv))
            E("s", lambda h, db=db: h.activation(xhq[:, db, :], taps[:, 0, :], AF.Silu,
                                                 bias=cb_c[:, db:db + 1]), "ss")
        W("y", ("ss", CNT["ss"]))
        DMA("y", xh_sp[:, :, sl], xhq[:, :, :])
        tb([4])
        W("t", ("ss", CNT["ss"]))
        tv = E("t", lambda h: [h.matmul(ps[4][0:DS, :], wx_sb[:, db, :], xhq[:, db, :],
                                        start=(db == 0), stop=(db == DIC - 1))
                               for db in range(DIC)][-1], "st")
        W("s", ("st", tv))
        sv = E("s", lambda h, sl=sl: h.activation(B_sb[:, sl], ps[4][0:DS, :], AF.Identity,
                                                  bias=bx_c[:, :]), "ss")
        guard[4] = ("ss", sv)
        dvm = {}
        for m in range(2):
            dvm[m] = DMA("g", wdt_p[m % 2][:, :, :],
                         wdt_h[:, m * P:(m + 1) * P].rearrange("(kc p) n -> p kc n", p=P),
                         q="dg%d" % (m % 2))
        for m in range(DIC):
            bk = m % 4
            tb([bk])
            W("t", ("dg%d" % (m % 2), dvm[m]))
            def tf(h, m=m, bk=bk):
                last = None
                for kc in range(DIC):
                    last = h.matmul(ps[bk][:, :], wdt_p[m % 2][:, kc, :], xhq[:, kc, :],
                                    start=(kc == 0), stop=(kc == DIC - 1))
                return last
            tv = E("t", tf, "st")
            if m + 2 < DIC:
                W("g", ("st", tv))
                dvm[m + 2] = DMA("g", wdt_p[m % 2][:, :, :],
                                 wdt_h[:, (m + 2) * P:(m + 3) * P].rearrange(
                                     "(kc p) n -> p kc n", p=P), q="dg%d" % (m % 2))
            W("s", ("st", tv))
            def sf(h, m=m, bk=bk):
                h.activation(af(2048, TC), ps[bk][:, :], AF.Exp, bias=bdt_c[:, m:m + 1])
                return h.activation(dlq[:, m, :], af(2048, TC), AF.Ln, bias=scb[:, 12:13])
            sv = E("s", sf, "ss")
            guard[bk] = ("ss", sv)
            W("v", ("ss", sv))
            E("v", lambda h, m=m: h.tensor_tensor(duq[:, m, :], dlq[:, m, :],
                                                  xhq[:, m, :], OP.mult), "sv")
        W("y", ("ss", CNT["ss"]), ("sv", CNT["sv"]))
        DMA("y", dl_sp[:, :, sl], dlq[:, :, :])
        last_q_dmas = DMA("y", du_sp[:, :, sl], duq[:, :, :])
    if stop("ph5"):
        return finish()

    bar("t", "v", "s", "g", "y")
    bb8 = ab3(0, 8, T)
    dudb = ab3(16384, 2, T)
    hout = ab3(20480, 2, T)
    xhdb = ab3(24576, 2, T)
    ddb = a63(0, 2, T)
    apng = a63(4096, 2, T)
    y16 = a63(8192, 2, T)
    d16 = a63(12288, 2, T)

    sel_all = AB[0:DS, 30720:32768]
    gv = E("g", lambda h: (
        h.iota(AFr[0:DS, 0:2048], [[0, 2048]], channel_multiplier=1,
               allow_small_or_imprecise_dtypes=True),
        h.iota(AFr[0:DS, 2048:4096], [[1, DS], [0, P]], channel_multiplier=0,
               allow_small_or_imprecise_dtypes=True))[1], "sg")
    W("v", ("sg", gv))
    vsel0 = E("v", lambda h: h.tensor_tensor(sel_all, AFr[0:DS, 0:2048],
                                             AFr[0:DS, 2048:4096], OP.is_equal), "sv")
    last_iter_sv = {}
    hend_sg = {}
    for sgp in range(2):
        for si in range(8):
            s = sgp * 8 + si
            for tcb in range(NT):
                bk = 4 + (tcb % 2)
                tb([bk])
                W("t", ("sv", vsel0))
                def tf(h, s=s, tcb=tcb, bk=bk):
                    return h.matmul(ps[bk][:, :], AB[0:DS, 30720 + s * P:30720 + (s + 1) * P],
                                    B_sb[:, tcb * TC:(tcb + 1) * TC],
                                    start=True, stop=True)
                tv = E("t", tf, "st")
                W("s", ("st", tv), ("sv", CNT["sv"]))
                sv = E("s", lambda h, si=si, tcb=tcb, bk=bk:
                       h.copy(bb8[:, si, tcb * TC:(tcb + 1) * TC], ps[bk][:, :]), "ss")
                guard[bk] = ("ss", sv)
        bar("v")
        for db in range(DIC):
            pp = db % 2
            qd = "y" if pp == 0 else "s"
            qsem = QS[qd]
            gate = last_iter_sv.get((sgp, db - 2)) or (last_iter_sv.get((sgp - 1, DIC - 2 + db)) if sgp > 0 else None)
            if gate:
                W(qd, ("sv", gate))
            d1 = DMA(qd, ddb[:, pp, :], dl_sp[:, db, :])
            d2 = DMA(qd, dudb[:, pp, :], du_sp[:, db, :])
            d3 = None
            if sgp == 0:
                d3 = DMA(qd, xhdb[:, pp, :], xh_sp[:, db, :])
            else:
                d3 = DMA(qd, y16[:, pp, :], y_sp[:, db, :])
            dlast = d3
            W("v", (qsem, dlast))
            if sgp == 0:
                def vf(h, db=db, pp=pp):
                    h.tensor_tensor_scan(af(4096, T), ones_b[:, 0:1].broadcast_to([P, T]),
                                         ddb[:, pp, :], 0.0, OP.mult, OP.add)
                    return h.tensor_scalar(af(pp * T, T), xhdb[:, pp, :],
                                           dvec_c[:, db:db + 1], None, OP.mult)
                vcum = E("v", vf, "sv")
                vcv = E("v", lambda h, pp=pp: h.tensor_copy(d16[:, pp, :], af(4096, T)),
                        "sv")
                W(qd, ("sv", vcv))
                DMA(qd, dc_sp[:, db, :], d16[:, pp, :])
                def vpre(h, db=db):
                    for qq in range(1, 4):
                        h.tensor_scalar(dpre_sb[:, qq, db * DS:(db + 1) * DS],
                                        msk_c[:, 0:DS], 0.0, af(4096 + qq * TC - 1, 1),
                                        OP.mult, OP.add)
                    return h.tensor_scalar(dtotT[:, db * DS:(db + 1) * DS],
                                           msk_c[:, 0:DS], 0.0, af(4096 + T - 1, 1),
                                           OP.mult, OP.add)
                E("v", vpre, "sv")
            else:
                vcum = E("v", lambda h, db=db, pp=pp:
                         h.tensor_copy(af(pp * T, T), y16[:, pp, :]), "sv")
            for si in range(8):
                s = sgp * 8 + si
                prevv = last_iter_sv.get((sgp, db))
                W("s", (qsem, dlast))
                if (sgp, db, si - 2) in last_iter_sv or True:
                    pass
                gate_a = None
                if si >= 2:
                    gate_a = last_iter_sv[(sgp, db, si - 2)]
                elif db >= 1:
                    gate_a = last_iter_sv.get((sgp, db - 1, 6 + si))
                elif sgp == 1:
                    gate_a = last_iter_sv.get((0, DIC - 1, 6 + si))
                if gate_a:
                    W("s", ("sv", gate_a))
                sva = E("s", lambda h, db=db, s=s, si=si, pp=pp:
                        h.activation(apng[:, si % 2, :], ddb[:, pp, :], AF.Exp,
                                     scale=A_sl[:, db * DS + s:db * DS + s + 1]), "ss")
                W("v", ("ss", sva))
                gate_h = None
                if si >= 2:
                    gate_h = hend_sg.get((sgp, db, si - 2))
                elif db >= 1:
                    gate_h = hend_sg.get((sgp, db - 1, 6 + si))
                elif sgp == 1:
                    gate_h = hend_sg.get((0, DIC - 1, 6 + si))
                if gate_h:
                    W("v", ("sg", gate_h))
                def vf2(h, db=db, s=s, si=si, pp=pp):
                    h.tensor_tensor(ab(28672, T), dudb[:, pp, :], bb8[:, si, :], OP.mult)
                    h.tensor_tensor_scan(hout[:, si % 2, :], apng[:, si % 2, :],
                                         ab(28672, T), 0.0, OP.mult, OP.add)
                    return h.tensor_tensor(af(pp * T, T), af(pp * T, T),
                                           hout[:, si % 2, :], OP.add)
                vv = E("v", vf2, "sv")
                last_iter_sv[(sgp, db, si)] = vv
                W("g", ("sv", vv))
                gv = E("g", lambda h, db=db, s=s, si=si:
                       h.tensor_copy(hend_sb[:, db * DS + s:db * DS + s + 1],
                                     hout[:, si % 2, T - 1:T]), "sg")
                hend_sg[(sgp, db, si)] = gv
            last_iter_sv[(sgp, db)] = last_iter_sv[(sgp, db, 7)]
            vcv = E("v", lambda h, pp=pp: h.tensor_copy(y16[:, pp, :], af(pp * T, T)),
                    "sv")
            W(qd, ("sv", vcv))
            DMA(qd, y_sp[:, db, :], y16[:, pp, :])
            last_iter_sv[(sgp, db)] = vcv
    bar("v")
    vv = E("v", lambda h: h.tensor_tensor(af(0, 256), A_sl[:, :], dtotT[:, :],
                                          OP.mult), "sv")
    W("s", ("sv", vv))
    E("s", lambda h: h.activation(gam_sb[:, :], af(0, 256), AF.Exp), "ss")
    if stop("ph6"):
        return finish()

    bar("t", "v", "s", "g", "y")
    xq = ab3(0, 8, TC)
    gsb = a63(0, IHC, TC)
    wgt = ab3(4096, 8, IH)
    hspill = 0
    for q in range(NT):
        sl = slice(q * TC, (q + 1) * TC)
        W("v", ("sg", CNT["sg"]), ("st", CNT["st"]))
        def vq(h, sl=sl):
            last = None
            for c in range(8):
                h.tensor_scalar(af(0, TC), xn[:, c, sl], scb[:, 7:8], RC, OP.mult, OP.add)
                h.tensor_scalar(af(512, TC), af(0, TC), -RC, 127.0, OP.add, OP.min)
                last = h.tensor_scalar(xq[:, c, :], af(512, TC), -128.0, None, OP.max)
            return last
        vv_xq = E("v", vq, "sv")
        for wsp, scc, is_gate in ((wgq_sp, gsc_c, True), (wuq_sp, usc_c, False)):
            W("s", ("st", CNT["st"]))
            dvw = 0
            for hc in range(8):
                dvw = DMA("s", wgt[:, hc, :], wsp[:, hc * P:(hc + 1) * P], transpose=True)
            for ic in range(IHC):
                bk = ic % 4
                tb([bk])
                W("t", ("da", dvw), ("sv", vv_xq))
                def tf(h, ic=ic, bk=bk):
                    last = None
                    for hc in range(8):
                        last = h.matmul(ps[bk][:, :], wgt[:, hc, ic * P:(ic + 1) * P],
                                        xq[:, hc, :], start=(hc == 0), stop=(hc == 7))
                    return last
                tv = E("t", tf, "st")
                if is_gate:
                    W("s", ("st", tv))
                    sv = E("s", lambda h, ic=ic, bk=bk:
                           h.activation(gsb[:, ic, :], ps[bk][:, :], AF.Silu,
                                        scale=gsc_c[:, ic:ic + 1]), "ss")
                    guard[bk] = ("ss", sv)
                else:
                    W("v", ("st", tv), ("ss", CNT["ss"]), ("dy", hspill))
                    hfo = 2048 + (ic % 4) * TC
                    def vf(h, q=q, ic=ic, bk=bk, hfo=hfo):
                        h.scalar_tensor_tensor(af(hfo, TC), ps[bk][:, :],
                                               usc_c[:, ic:ic + 1], gsb[:, ic, :],
                                               OP.mult, OP.mult)
                        return h.tensor_reduce(af(4704 + q * 22 + ic, 1), af(hfo, TC),
                                               AX.X, OP.max, apply_absolute_value=True)
                    vv = E("v", vf, "sv")
                    guard[bk] = ("sv", vv)
                    if ic % 4 == 3 or ic == IHC - 1:
                        W("y", ("sv", vv))
                        n = (ic % 4) + 1
                        hspill = DMA("y", h_sp[:, ic - n + 1:ic + 1, sl],
                                     af3(2048, n, TC))
    def vvd7(h):
        h.tensor_copy(AFr[:, 4800:4888], AFr[:, 4704:4792])
        h.tensor_copy(AFr[:, 4800:4888], AFr[:, 4704:4792])
        return h.tensor_reduce(xden[:, :], af(4704, 88), AX.X, OP.max)
    vvd = E("v", vvd7, "sv")
    W("g", ("sv", CNT["sv"]))
    gv = E("g", lambda h: h.tensor_reduce(invz[0:1, 0:1], xden[:, :], AX.C, OP.max),
           "sg")
    tb([5])
    W("t", ("sg", gv))
    tv = E("t", lambda h: h.matmul(ps[5][:, 0:1], ones_b[0:1, :], invz[0:1, 0:1],
                                   start=True, stop=True), "st")
    W("v", ("st", tv))
    vv = E("v", lambda h: h.tensor_copy(xm_col[:, 1:2], ps[5][:, 0:1]), "sv")
    guard[5] = ("sv", vv)
    if stop("ph7"):
        return finish()

    bar("t", "v", "s", "g", "y")
    def vpack(h):
        h.tensor_copy(af(0, 256), gam_sb[:, :])
        h.tensor_copy(af(256, 256), hend_sb[:, :])
        return h.tensor_copy(af(512, 1), xm_col[:, 1:2])
    vv = E("v", vpack, "sv")
    W("y", ("sv", vv))
    dv = DMA("y", ag2_in[:, 0:513], af(0, 513))
    W("g", ("dy", dv))
    cv = E("g", lambda h: h.collective_compute(
        "AllGather", OP.bypass, replica_groups=[list(range(NC8))],
        ins=[ag2_in.opt()], outs=[ag2_out.opt()]), "cc")
    W("g", ("cc", cv))
    dv2 = DMA("g", af3(1024, 8, 516), ag2_out[:, :, :].rearrange("j p f -> p j f"))
    W("v", ("dg", dv2), ("sg", CNT["sg"]))
    SUF, TMP, TM2 = 5152, 5408, 5664
    def vcomb(h):
        agt = af3(1024, 8, 516)
        h.memset(af(SUF, 256), 1.0)
        h.memset(hin_sb[:, :], 0.0)
        for j in range(6, -1, -1):
            h.tensor_scalar(af(TMP, 256), agt[:, j + 1, 0:256], msk_c[:, 8 + j + 1:10 + j],
                            omsk_c[:, 8 + j + 1:10 + j], OP.mult, OP.add)
            h.tensor_tensor(af(SUF, 256), af(SUF, 256), af(TMP, 256), OP.mult)
            h.tensor_tensor(af(TM2, 256), agt[:, j, 256:512], af(SUF, 256), OP.mult)
            h.scalar_tensor_tensor(hin_sb[:, :], af(TM2, 256), msk_c[:, 8 + j:9 + j],
                                   hin_sb[:, :], OP.mult, OP.add)
        h.tensor_copy(af(5920, 8), agt[:, :, 512])
        h.tensor_copy(af(5940, 8), agt[:, :, 512])
        h.tensor_copy(af(5940, 8), agt[:, :, 512])
        h.tensor_reduce(xm_col[:, 3:4], af(5920, 8), AX.X, OP.max)
        h.tensor_copy(af(5940, 8), agt[:, :, 512])
        h.tensor_copy(af(5940, 8), agt[:, :, 512])
        h.tensor_scalar(af(TMP, 1), scb[:, 6:7], xm_col[:, 3:4], None, OP.mult)
        h.tensor_copy(af(5940, 8), agt[:, :, 512])
        h.tensor_copy(af(5940, 8), agt[:, :, 512])
        h.reciprocal(af(TMP + 1, 1), af(TMP, 1))
        h.tensor_copy(af(5940, 8), agt[:, :, 512])
        h.tensor_copy(af(5940, 8), agt[:, :, 512])
        h.tensor_scalar(scb[:, 9:10], af(TMP + 1, 1), 127.0, None, OP.mult)
        h.tensor_scalar(scb[:, 10:11], af(TMP, 1), 1.0 / 127.0, None, OP.mult)
        h.tensor_copy(af(5940, 8), agt[:, :, 512])
        h.tensor_copy(af(5940, 8), agt[:, :, 512])
        return h.tensor_scalar(dsc_c[:, :], sd_c[:, :], scb[:, 10:11], None, OP.mult)
    vv2 = E("v", vcomb, "sv")
    if stop("ph8"):
        return finish()

    bar("t", "v", "s", "g", "y")
    dvbt = DMA("g", rowB[:, :], bout_h[None, :])
    yqa = a63(0, 16, TC)
    dqa = a63(8192, 16, TC)
    corrp = ab3(0, 2, TC)
    ygall = ab3(2048, 16, TC)
    zdbuf = ab(10240, TC)
    winz_p = [ab3(10752, 8, P), ab3(11776, 8, P)]
    woutb = ab3(12800, DIC, DM)
    hq = ab3(0, IHC, TC)
    wdh = ab3(11264, 11, 1024)
    for q in range(NT):
        sl = slice(q * TC, (q + 1) * TC)
        bar("y", "s", "g")
        d1 = DMA("y", yqa[:, :, :], y_sp[:, :, sl])
        d2 = DMA("y", dqa[:, :, :], dc_sp[:, :, sl])
        dvo = DMA("g", woutb[:, :, :], wout_h.rearrange("(kc p) m -> p kc m", p=P))
        if q == 0:
            E("v", lambda h: h.tensor_copy(hinq1[:, :], hin_sb[:, :]), "sv")
        else:
            vq3 = E("v", lambda h, q=q: h.tensor_tensor(af(1024, 256), A_sl[:, :],
                                                        dpre_sb[:, q, :], OP.mult), "sv")
            W("s", ("sv", vq3))
            svq = E("s", lambda h: h.activation(af(1280, 256), af(1024, 256), AF.Exp),
                    "ss")
            W("v", ("ss", svq))
            E("v", lambda h: h.tensor_tensor(hinq1[:, :], hin_sb[:, :],
                                             af(1280, 256), OP.mult), "sv")
        corr_sv = {}
        W("s", ("dy", d2), ("sv", CNT["sv"]))
        for db in range(DIC):
            for s in range(DS):
                gate_a = None
                if s >= 2:
                    gate_a = corr_sv[(db, s - 2)]
                elif db >= 1:
                    gate_a = corr_sv[(db - 1, DS - 2 + s)]
                if gate_a:
                    W("s", ("sv", gate_a))
                sva = E("s", lambda h, db=db, s=s:
                        h.activation(corrp[:, s % 2, :], dqa[:, db, :], AF.Exp,
                                     scale=A_sl[:, db * DS + s:db * DS + s + 1]), "ss")
                W("v", ("ss", sva), ("dy", d2))
                vv = E("v", lambda h, q=q, db=db, s=s:
                       h.scalar_tensor_tensor(yqa[:, db, :], corrp[:, s % 2, :],
                                              hinq1[:, db * DS + s:db * DS + s + 1],
                                              yqa[:, db, :], OP.mult, OP.add), "sv")
                corr_sv[(db, s)] = vv
        dvz = {}
        for db in range(2):
            dvz[db] = DMA("g", winz_p[db % 2][:, :, :],
                          win_h[:, DI + db * P:DI + (db + 1) * P].rearrange(
                              "(kc p) n -> p kc n", p=P), q="dg%d" % (db % 2))
        for db in range(DIC):
            bk = db % 2
            tb([bk])
            W("t", ("dg%d" % (db % 2), dvz[db]), ("sv", CNT["sv"]))
            def tf(h, db=db, bk=bk, sl=sl):
                last = None
                for kc in range(8):
                    last = h.matmul(ps[bk][:, :], winz_p[db % 2][:, kc, :],
                                    xn[:, kc, sl], start=(kc == 0), stop=(kc == 7))
                return last
            tv = E("t", tf, "st")
            if db + 2 < DIC:
                W("g", ("st", tv))
                dvz[db + 2] = DMA("g", winz_p[db % 2][:, :, :],
                                  win_h[:, DI + (db + 2) * P:DI + (db + 3) * P].rearrange(
                                      "(kc p) n -> p kc n", p=P), q="dg%d" % (db % 2))
            W("s", ("st", tv), ("sv", CNT["sv"]))
            sv = E("s", lambda h, db=db, bk=bk:
                   h.activation(zdbuf[:, :], ps[bk][:, :], AF.Silu,
                                bias=binz_c[:, db:db + 1]), "ss")
            guard[bk] = ("ss", sv)
            W("v", ("ss", sv))
            E("v", lambda h, db=db: h.tensor_tensor(ygall[:, db, :], yqa[:, db, :],
                                                    zdbuf[:, :], OP.mult), "sv")
        W("t", ("dg", dvo), ("dg", dvbt), ("sv", CNT["sv"]))
        for half in range(2):
            tb([0, 1, 2, 3])
            def tf(h, half=half):
                for db in range(DIC):
                    for j in range(4):
                        oc = half * 4 + j
                        h.matmul(ps[j][:, :], woutb[:, db, oc * P:(oc + 1) * P],
                                 ygall[:, db, :], start=(db == 0), stop=False)
                last = None
                for j in range(4):
                    oc = half * 4 + j
                    last = h.matmul(ps[j][:, :], rowB[0:1, oc * P:(oc + 1) * P],
                                    onesr_b[0:1, :], start=False, stop=True)
                return last
            tv = E("t", tf, "st")
            W("v", ("st", tv))
            def vf(h, half=half, sl=sl):
                last = None
                for j in range(4):
                    oc = half * 4 + j
                    last = h.scalar_tensor_tensor(res[:, oc, sl], ps[j][:, :], 1.0,
                                                  res[:, oc, sl], OP.mult, OP.add)
                return last
            vv = E("v", vf, "sv")
            for j in range(4):
                guard[j] = ("sv", vv)
        d4last = 0
        for grp, ics in enumerate(tuple((i, min(i + 2, IHC)) for i in range(0, IHC, 2))):
            lo, hi = ics
            W("y", ("sv", CNT["sv"]))
            d4 = DMA("y", af3(0, hi - lo, TC), h_sp[:, lo:hi, sl])
            W("v", ("dy", d4), ("sv", CNT["sv"]))
            def vf(h, lo=lo, hi=hi):
                last = None
                for i in range(hi - lo):
                    h.tensor_scalar(af(1024, TC), af3(0, hi - lo, TC)[:, i, :],
                                    scb[:, 9:10], RC, OP.mult, OP.add)
                    h.tensor_scalar(af(1536, TC), af(1024, TC), -RC, 127.0,
                                    OP.add, OP.min)
                    last = h.tensor_scalar(hq[:, lo + i, :], af(1536, TC), -128.0,
                                           None, OP.max)
                return last
            E("v", vf, "sv")
        for wh in range(2):
            ic0, ic1 = (0, 11) if wh == 0 else (11, 22)
            W("s", ("st", CNT["st"]), ("sv", CNT["sv"]))
            dvd = 0
            for i in range(ic0, ic1):
                dvd = DMA("s", wdh[:, i - ic0, :], wdq_sp[:, i * P:(i + 1) * P],
                          transpose=True)
            for oc in range(8):
                bk = [0, 1, 4, 5][oc % 4]
                tb([bk])
                W("t", ("da", dvd), ("sv", CNT["sv"]))
                def tf(h, wh=wh, oc=oc, bk=bk, ic0=ic0, ic1=ic1):
                    last = None
                    for i in range(ic0, ic1):
                        last = h.matmul(ps[bk][:, :], wdh[:, i - ic0, oc * P:(oc + 1) * P],
                                        hq[:, i, :], start=(i == ic0), stop=(i == ic1 - 1))
                    return last
                tv = E("t", tf, "st")
                W("v", ("st", tv))
                if wh == 0:
                    vv = E("v", lambda h, oc=oc, bk=bk:
                           h.tensor_copy(af(2048 + oc * TC, TC), ps[bk][:, :]), "sv")
                else:
                    def vf(h, oc=oc, bk=bk, sl=sl):
                        h.tensor_tensor(af(1024, TC), ps[bk][:, :], af(2048 + oc * TC, TC),
                                        OP.add)
                        return h.scalar_tensor_tensor(res[:, oc, sl], af(1024, TC),
                                                      dsc_c[:, oc:oc + 1],
                                                      res[:, oc, sl], OP.mult, OP.add)
                    vv = E("v", vf, "sv")
                guard[bk] = ("sv", vv)
        for tsub in range(4):
            bk = 6 + (tsub % 2)
            tb([bk])
            W("t", ("sv", CNT["sv"]))
            def tf(h, q=q, tsub=tsub, bk=bk):
                last = None
                for dc in range(8):
                    last = h.transpose(pbs[bk - 6][:, dc * P:(dc + 1) * P],
                                       res[:, dc, q * TC + tsub * P:q * TC + (tsub + 1) * P],
                                       eye_b[:, :])
                return last
            tv = E("t", tf, "st")
            W("s", ("st", tv), ("dy", CNT["dy"]))
            sv = E("s", lambda h, bk=bk:
                   h.copy(af(0, 1024), pbs[bk - 6][:, :]), "ss")
            guard[bk] = ("ss", sv)
            W("y", ("ss", sv))
            DMA("y", out_h[q * TC + tsub * P:q * TC + (tsub + 1) * P, :],
                af(0, 1024))
    return finish()


def kernel(**inputs):
    gi = lambda k: np.ascontiguousarray(np.asarray(inputs[k]))
    x = gi("L0_attn_x").reshape(16384, 1024).astype(np.float32)
    sc8 = np.zeros(8, np.float32)
    sc8[0] = np.float32(inputs["L0_ffn_s0"])
    sc8[1] = np.float32(inputs["L0_ffn_s1"])
    sc8[2] = np.float32(inputs["L0_ffn_s2"])
    sc8[3] = np.asarray(inputs["L1_ffn_clip_x"], np.float32).reshape(-1)[0]
    sc8[4] = np.asarray(inputs["L1_ffn_clip_down"], np.float32).reshape(-1)[0]
    shared = dict(
        wq=gi("L0_attn_Wq"), wk=gi("L0_attn_Wk"), wv=gi("L0_attn_Wv"), wo=gi("L0_attn_Wo"),
        bq=gi("L0_attn_bq"), bk=gi("L0_attn_bk"), bv=gi("L0_attn_bv"), bo=gi("L0_attn_bo"),
        k0=gi("L0_ffn_k0").view(np.uint8), k1=gi("L0_ffn_k1").view(np.uint8),
        k2=gi("L0_ffn_k2").view(np.uint8), sc8=sc8,
        win=gi("L1_attn_W_in"), bin=gi("L1_attn_b_in"),
        cw=np.ascontiguousarray(gi("L1_attn_conv_w")[:, 0, :]),
        cb=gi("L1_attn_conv_b"),
        wx=np.ascontiguousarray(gi("L1_attn_W_x")[:, 16:32]),
        bx=np.ascontiguousarray(gi("L1_attn_b_x")[16:32]),
        wdt=gi("L1_attn_W_dt"), bdt=gi("L1_attn_b_dt"),
        alog=gi("L1_attn_A_log"), dvec=gi("L1_attn_D"),
        wout=gi("L1_attn_W_out"), bout=gi("L1_attn_b_out"),
        wg=gi("L1_ffn_w_gate"), wu=gi("L1_ffn_w_up"), wd=gi("L1_ffn_w_down"),
    )
    nc = build(STOP_AFTER, skip=SKIP)
    in_maps = []
    for c in range(8):
        m = dict(shared)
        m["x"] = np.ascontiguousarray(x[c * 2048:(c + 1) * 2048])
        msk = np.zeros(16, np.float32)
        if c > 0:
            msk[c - 1] = 1.0
        msk[8:8 + c] = 1.0
        m["msk"] = msk
        in_maps.append(m)
    res = run_bass_kernel_spmd(nc, in_maps, core_ids=list(range(8)))
    if STOP_AFTER is not None:
        return res
    out = np.concatenate([res.results[c]["y"] for c in range(8)], axis=0)
    return out.reshape(1, 16384, 1024).astype(np.float32)


# version 04
# speedup vs baseline: 1.1042x; 1.1042x over previous
import numpy as np
import concourse.bass as bass
import concourse.mybir as mybir
import bass_rust
from concourse.bass_utils import run_bass_kernel_spmd

F32 = mybir.dt.float32
BF16 = mybir.dt.bfloat16
F16 = mybir.dt.float16
U8 = mybir.dt.uint8
AF = mybir.ActivationFunctionType
OP = mybir.AluOpType
AX = mybir.AxisListType

P = 128
T = 2048
NT = 4
TC = 512
DM, DC = 1024, 8
DI, DIC = 2048, 16
DS = 16
FF = 4096
IH, IHC = 2816, 22
NC8 = 8
RC = 12582912.0
EPS = 1e-5

STOP_AFTER = None
SKIP = set()


def build(stop_after=None, skip=()):
    nc = bass.Bass("TRN2", target_bir_lowering=False)

    ein = lambda n, s, d=F32: nc.dram_tensor(n, s, d, kind="ExternalInput").ap()
    x_h = ein("x", [T, DM])
    wq_h, wk_h = ein("wq", [DM, DM]), ein("wk", [DM, DM])
    wv_h, wo_h = ein("wv", [DM, DM]), ein("wo", [DM, DM])
    bq_h, bk_h = ein("bq", [DM]), ein("bk", [DM])
    bv_h, bo_h = ein("bv", [DM]), ein("bo", [DM])
    k0_h, k1_h, k2_h = ein("k0", [2 * DM, FF], U8), ein("k1", [FF, FF], U8), ein("k2", [FF, DM], U8)
    sc8_h = ein("sc8", [8])
    win_h, bin_h = ein("win", [DM, 2 * DI]), ein("bin", [2 * DI])
    cw_h, cb_h = ein("cw", [DI, 4]), ein("cb", [DI])
    wx_h, bx_h = ein("wx", [DI, DS]), ein("bx", [DS])
    wdt_h, bdt_h = ein("wdt", [DI, DI]), ein("bdt", [DI])
    alog_h, dvec_h = ein("alog", [DI, DS]), ein("dvec", [DI])
    wout_h, bout_h = ein("wout", [DI, DM]), ein("bout", [DM])
    wg_h, wu_h, wd_h = ein("wg", [IH, DM]), ein("wu", [IH, DM]), ein("wd", [DM, IH])
    msk_h = ein("msk", [16])

    out_h = nc.dram_tensor("y", [T, DM], F32, kind="ExternalOutput").ap()
    dbg = {}
    if stop_after == "ph4":
        dbg["cols"] = nc.dram_tensor("dbg_cols", [P, 33], F32, kind="ExternalOutput").ap()
    if stop_after is not None:
        dbg["res"] = nc.dram_tensor("dbg_res", [P, DC, T], BF16, kind="ExternalOutput").ap()
        dbg["xn"] = nc.dram_tensor("dbg_xn", [P, DC, T], BF16, kind="ExternalOutput").ap()
        dbg["smalls"] = nc.dram_tensor("dbg_smalls", [P, 2048], F32, kind="ExternalOutput").ap()
    if stop_after in ("ph5", "ph6", "ph7"):
        dbg["s1"] = nc.dram_tensor("dbg_s1", [P, DIC, T], BF16, kind="ExternalOutput").ap()
        dbg["s2"] = nc.dram_tensor("dbg_s2", [P, DIC, T], BF16, kind="ExternalOutput").ap()
        dbg["s5"] = nc.dram_tensor("dbg_s5", [P, DIC, T], BF16, kind="ExternalOutput").ap()
        dbg["s3"] = nc.dram_tensor("dbg_s3", [P, DIC, T], F16, kind="ExternalOutput").ap()
        dbg["s4"] = nc.dram_tensor("dbg_s4", [P, DIC, T], F16, kind="ExternalOutput").ap()
    if stop_after == "ph7":
        dbg["s6"] = nc.dram_tensor("dbg_s6", [P, IHC, T], F32, kind="ExternalOutput").ap()

    scr = lambda n, s, d: nc.dram_tensor(n, s, d).ap()
    kv_in = scr("kv_in", [P, DC, DM + 1], F32)
    kv_out = scr("kv_out", [P, DC, DM + 1], F32)
    ag1_in = scr("ag1_in", [P, 52], F32)
    ag1_out = scr("ag1_out", [NC8, P, 52], F32)
    ag2_in = scr("ag2_in", [P, 516], F32)
    ag2_out = scr("ag2_out", [NC8, P, 516], F32)
    xn1_sp = scr("xn1_sp", [P, DC, T], BF16)
    xhp_sp = scr("xhp_sp", [P, DIC, T], BF16)
    xh_sp = scr("xh_sp", [P, DIC, T], BF16)
    dl_sp = scr("dl_sp", [P, DIC, T], F16)
    du_sp = scr("du_sp", [P, DIC, T], BF16)
    y_sp = scr("y_sp", [P, DIC, T], F16)
    dc_sp = scr("dc_sp", [P, DIC, T], F16)
    h_sp = scr("h_sp", [P, IHC, T], F32)
    wgq_sp = scr("wgq_sp", [IH, DM], BF16)
    wuq_sp = scr("wuq_sp", [IH, DM], BF16)
    wdq_sp = scr("wdq_sp", [DM, IH], BF16)

    from contextlib import ExitStack
    es = ExitStack()
    SEMN = ["st", "sv", "ss", "sg", "dg", "dy", "da", "cc", "dg0", "dg1"]
    SEM = {n: es.enter_context(nc.semaphore(n)) for n in SEMN}
    CNT = {n: 0 for n in SEMN}
    blk = es.enter_context(nc.Block())
    OWN = dict(t="st", v="sv", s="ss", g="sg", y=None)
    QS = dict(g="dg", y="dy", s="da")
    streams = {e: [] for e in "tvsgy"}

    def W(e, *reqs):
        reqs = [(s, v) for (s, v) in reqs if v and v > 0]
        if reqs:
            streams[e].append(("w", reqs))

    def E(e, fn, sem=None, k=1):
        val = None
        if sem is not None:
            CNT[sem] += k
            val = CNT[sem]
        streams[e].append(("e", fn, sem, k))
        return val

    def DMA(e, out, in_, q=None, **kw):
        return E(e, lambda h, out=out, in_=in_, kw=kw: h.dma_start(out=out, in_=in_, **kw),
                 sem=(q or QS[e]), k=16)

    def bar(*engs):
        snap = dict(CNT)
        for e in engs:
            W(e, *[(n, v) for n, v in snap.items() if n != OWN[e]])

    sb = lambda n, s, d: es.enter_context(nc.sbuf_tensor(n, s, d))
    res = sb("res", [P, DC, T], BF16)
    xn = sb("xn", [P, DC, T], BF16)
    AB = sb("AB", [P, 32768], BF16)
    A16 = sb("A16", [P, 16384], F16)
    AFr = sb("AFr", [P, 6144], F32)

    eye_b = sb("eye_b", [P, P], BF16)
    ones_b = sb("ones_b", [P, P], BF16)
    onesr_b = sb("onesr_b", [1, TC], BF16)
    bq_c = sb("bq_c", [P, DC], F32)
    rowA = sb("rowA", [1, DM], BF16)
    rowB = sb("rowB", [1, DM], BF16)
    binxh_c = sb("binxh_c", [P, DIC], F32)
    binz_c = sb("binz_c", [P, DIC], F32)
    cw_c = sb("cw_c", [P, DIC, 4], F32)
    cb_c = sb("cb_c", [P, DIC], F32)
    bdt_c = sb("bdt_c", [P, DIC], F32)
    dvec_c = sb("dvec_c", [P, DIC], F32)
    bx_c = sb("bx_c", [DS, 1], F32)
    wx_sb = sb("wx_sb", [P, DIC, DS], BF16)
    A_sl = sb("A_sl", [P, 256], F32)
    sc_row = sb("sc_row", [1, 8], F32)
    msk_row = sb("msk_row", [1, 16], F32)
    scb = sb("scb", [P, 16], F32)
    msk_c = sb("msk_c", [P, 16], F32)
    omsk_c = sb("omsk_c", [P, 16], F32)
    gam_sb = sb("gam_sb", [P, 256], F32)
    hend_sb = sb("hend_sb", [P, 256], BF16)
    hin_sb = sb("hin_sb", [P, 256], F32)
    hinq1 = sb("hinq1", [P, 256], F32)
    dpre_sb = sb("dpre_sb", [P, 4, 256], F16)
    dtotT = sb("dtotT", [P, 256], F32)
    ksum_b = sb("ksum_b", [P, DC], BF16)
    B_sb = sb("B_sb", [DS, T], BF16)
    sg_c = sb("sg_c", [P, IHC], F32)
    su_c = sb("su_c", [P, IHC], F32)
    sd_c = sb("sd_c", [P, DC], F32)
    gsc_c = sb("gsc_c", [P, IHC], F32)
    usc_c = sb("usc_c", [P, IHC], F32)
    dsc_c = sb("dsc_c", [P, DC], F32)
    tail_sb = sb("tail_sb", [P, DIC, 3], BF16)
    invz = sb("invz", [1, TC], BF16)
    xm_col = sb("xm_col", [P, 4], F32)
    xden = sb("xden", [P, 1], F32)

    ps = [es.enter_context(nc.psum_tensor(f"ps{i}", [P, TC], F32)) for i in range(6)]
    pbs = [es.enter_context(nc.psum_tensor(f"pb{i}", [P, 1024], BF16)) for i in range(2)]

    def ab(off, n):
        assert off + n <= 32768
        return AB[:, off:off + n]

    def ab3(off, c, l):
        assert off + c * l <= 32768
        return AB[:, off:off + c * l].rearrange("p (c l) -> p c l", c=c)

    def a6(off, n):
        assert off + n <= 16384
        return A16[:, off:off + n]

    def a63(off, c, l):
        assert off + c * l <= 16384
        return A16[:, off:off + c * l].rearrange("p (c l) -> p c l", c=c)

    def af(off, n):
        assert off + n <= 6144
        return AFr[:, off:off + n]

    def af3(off, c, l):
        assert off + c * l <= 6144
        return AFr[:, off:off + c * l].rearrange("p (c l) -> p c l", c=c)

    guard = {i: None for i in range(8)}

    def tb(banks):
        reqs = [guard[b] for b in banks if guard[b] is not None]
        if reqs:
            W("t", *reqs)

    def finish():
        fin = [(n, v) for n, v in CNT.items() if v > 0]
        streams["y"].append(("w", fin))
        EM = dict(t=blk.tensor, v=blk.vector, s=blk.scalar, g=blk.gpsimd, y=blk.sync)
        for e, items in streams.items():
            def body(h, items=items):
                for it in items:
                    if it[0] == "w":
                        for sn, val in it[1]:
                            h.wait_ge(SEM[sn], val)
                    else:
                        _, fn, sem, k = it
                        ins = fn(h)
                        if sem is not None:
                            ins.then_inc(SEM[sem], k)
            EM[e](body)
        es.close()
        return nc

    def stop(name):
        if stop_after == name:
            bar("y", "v")
            def vpk(h):
                h.tensor_copy(AFr[:, 0:256], gam_sb[:, :])
                h.tensor_copy(AFr[:, 256:512], hend_sb[:, :])
                h.tensor_copy(AFr[:, 512:768], hin_sb[:, :])
                h.tensor_copy(AFr[:, 768:1024], dtotT[:, :])
                h.tensor_copy(AFr[:, 1024:1280], dpre_sb[:, 1, :])
                h.tensor_copy(AFr[:, 1280:1296], scb[:, :])
                h.tensor_copy(AFr[:, 1296:1300], xm_col[:, :])
                h.tensor_copy(AFr[:, 1300:1332], AFr[:, 2560:2592])
                h.tensor_copy(AFr[:, 1332:1333], xden[:, :])
                h.tensor_copy(AFr[:, 1333:1421], AFr[:, 4704:4792])
                return h.tensor_copy(AFr[:, 1400:1816], af3(1024, 8, 52))
            vv = E("v", vpk, "sv")
            W("y", ("sv", vv))
            DMA("y", dbg["smalls"], AFr[:, 0:2048])
            DMA("y", dbg["res"], res[:, :, :])
            DMA("y", dbg["xn"], xn[:, :, :])
            if name in ("ph5", "ph6", "ph7"):
                DMA("y", dbg["s1"], xhp_sp)
                DMA("y", dbg["s2"], xh_sp)
                DMA("y", dbg["s5"], du_sp)
                DMA("y", dbg["s3"], dl_sp if name == "ph5" else y_sp)
                DMA("y", dbg["s4"], dc_sp)
            if name == "ph7":
                DMA("y", dbg["s6"], h_sp)
            return True
        return False

    DMA("g", bq_c[:, :], bq_h.rearrange("(c p) -> p c", p=P), allow_slow_non_contiguous=True)
    DMA("g", rowA[:, :], bk_h[None, :])
    DMA("g", rowB[:, :], bv_h[None, :])
    DMA("g", binxh_c[:, :], bin_h[0:DI].rearrange("(c p) -> p c", p=P), allow_slow_non_contiguous=True)
    DMA("g", binz_c[:, :], bin_h[DI:2 * DI].rearrange("(c p) -> p c", p=P), allow_slow_non_contiguous=True)
    DMA("g", cw_c[:, :, :], cw_h.rearrange("(c p) k -> p c k", p=P))
    DMA("g", cb_c[:, :], cb_h.rearrange("(c p) -> p c", p=P), allow_slow_non_contiguous=True)
    DMA("g", bdt_c[:, :], bdt_h.rearrange("(c p) -> p c", p=P), allow_slow_non_contiguous=True)
    DMA("g", dvec_c[:, :], dvec_h.rearrange("(c p) -> p c", p=P), allow_slow_non_contiguous=True)
    DMA("g", bx_c[:, :], bx_h[:, None])
    DMA("g", wx_sb[:, :, :], wx_h.rearrange("(c p) s -> p c s", p=P))
    DMA("g", A_sl[:, :].rearrange("p (c s) -> p c s", c=DIC),
        alog_h.rearrange("(c p) s -> p c s", p=P))
    DMA("g", sc_row[:, :], sc8_h[None, :])
    dv0 = DMA("g", msk_row[:, :], msk_h[None, :])
    E("g", lambda h: h.iota(af(0, P), [[0, P]], channel_multiplier=1,
                            allow_small_or_imprecise_dtypes=True), "sg")
    E("g", lambda h: h.iota(af(P, P), [[1, P]], channel_multiplier=0,
                            allow_small_or_imprecise_dtypes=True), "sg")
    E("g", lambda h: h.memset(ones_b[:, :], 1.0), "sg")
    E("g", lambda h: h.memset(af(512, P), 1.0), "sg")
    E("g", lambda h: h.memset(scb[:, 11:12], EPS), "sg")
    E("g", lambda h: h.memset(scb[:, 12:13], 1.0), "sg")
    E("g", lambda h: h.memset(onesr_b[:, :], 1.0), "sg")
    E("g", lambda h: h.memset(hin_sb[:, :], 0.0), "sg")
    E("g", lambda h: h.memset(dpre_sb[:, :, :], 0.0), "sg")
    E("g", lambda h: h.memset(tail_sb[:, :, :], 0.0), "sg")
    bar("v")
    E("v", lambda h: h.tensor_tensor(eye_b[:, :], af(0, P), af(P, P), OP.is_equal), "sv")
    bar("t")
    E("t", lambda h: h.matmul(ps[0][:, 0:8], af(512, P)[0:1, :], sc_row[0:1, :],
                              start=True, stop=True), "st")
    tv = E("t", lambda h: h.matmul(ps[0][:, 8:24], af(512, P)[0:1, :], msk_row[0:1, :],
                                   start=True, stop=True), "st")
    W("v", ("st", tv))
    def _c0(v):
        v.tensor_copy(msk_c[:, :], ps[0][:, 8:24])
        v.tensor_scalar(omsk_c[:, :], ps[0][:, 8:24], -1.0, 1.0, OP.mult, OP.add)
        v.tensor_scalar(scb[:, 0:1], ps[0][:, 0:1], 4.0, None, OP.mult)
        v.tensor_scalar(scb[:, 1:2], ps[0][:, 1:2], 2.0, None, OP.mult)
        v.tensor_scalar(scb[:, 2:3], ps[0][:, 1:2], 1.0, None, OP.mult)
        v.tensor_scalar(scb[:, 3:4], ps[0][:, 2:3], 2.0, None, OP.mult)
        v.tensor_scalar(scb[:, 4:5], ps[0][:, 2:3], 1.0, None, OP.mult)
        return v.tensor_copy(af(256, 2), ps[0][:, 3:5])
    vv = E("v", _c0, "sv")
    guard[0] = ("sv", vv)
    W("s", ("sv", vv))
    sv = E("s", lambda h: h.activation(af(260, 2), af(256, 2), AF.Exp, scale=-1.0), "ss")
    W("v", ("ss", sv))
    def _c1(v):
        v.tensor_scalar(af(264, 2), af(260, 2), 1.0, None, OP.add)
        v.tensor_copy(af(280, 8), af(256, 8))
        v.tensor_copy(af(280, 8), af(256, 8))
        return v.reciprocal(scb[:, 5:7], af(264, 2))
    E("v", _c1, "sv")
    bar("s")
    sv = E("s", lambda h: h.activation(A_sl[:, :], A_sl[:, :], AF.Exp), "ss")
    W("v", ("ss", sv))
    E("v", lambda h: h.tensor_scalar(A_sl[:, :], A_sl[:, :], -1.0, None, OP.mult), "sv")

    last_tp = 0
    for xh2 in range(2):
        if last_tp:
            W("g", ("st", last_tp))
        dvx = DMA("g", ab3(0, 8, 1024),
                  x_h[xh2 * 1024:(xh2 + 1) * 1024, :].rearrange("(tc p) d -> p tc d", p=P))
        W("t", ("dg", dvx))
        for dc in range(8):
            for tg in range(2):
                bk = 6 + (tg % 2)
                tb([bk])
                def tf(h, dc=dc, tg=tg, bk=bk):
                    last = None
                    for j in range(4):
                        last = h.transpose(pbs[bk - 6][:, j * P:(j + 1) * P],
                                           ab3(0, 8, 1024)[:, tg * 4 + j, dc * P:(dc + 1) * P],
                                           eye_b[:, :])
                    return last
                tv = E("t", tf, "st")
                last_tp = tv
                W("v", ("st", tv))
                vv = E("v", lambda h, dc=dc, tg=tg, bk=bk, xh2=xh2:
                       h.tensor_copy(res[:, dc, xh2 * 1024 + tg * 512:xh2 * 1024 + (tg + 1) * 512],
                                     pbs[bk - 6][:, 0:512]), "sv")
                guard[bk] = ("sv", vv)

    MEAN, M2, VAR, INV = 512, 1024, 1536, 2048

    def layernorm(dst, absmax_to=None):
        for tc in range(NT):
            sl = slice(tc * TC, (tc + 1) * TC)
            W("s", ("sv", CNT["sv"]))
            def sf(h, sl=sl):
                last = None
                for c in range(8):
                    last = h.activation(ab3(8192, 8, TC)[:, c, :], res[:, c, sl], AF.Square)
                return last
            sv = E("s", sf, "ss")
            tb([0, 1])
            W("t", ("ss", sv), ("sv", CNT["sv"]))
            def tf(h, sl=sl):
                for c in range(8):
                    h.matmul(ps[0][:, :], ones_b[:, :], res[:, c, sl],
                             start=(c == 0), stop=(c == 7))
                last = None
                for c in range(8):
                    last = h.matmul(ps[1][:, :], ones_b[:, :], ab3(8192, 8, TC)[:, c, :],
                                    start=(c == 0), stop=(c == 7))
                return last
            tv = E("t", tf, "st")
            W("v", ("st", tv))
            def vf(h):
                h.tensor_scalar(af(MEAN, TC), ps[0][:, :], 1.0 / DM, None, OP.mult)
                h.tensor_tensor(af(M2, TC), af(MEAN, TC), af(MEAN, TC), OP.mult)
                return h.scalar_tensor_tensor(af(VAR, TC), ps[1][:, :], 1.0 / DM,
                                              af(M2, TC), OP.mult, OP.subtract)
            vv = E("v", vf, "sv")
            W("s", ("sv", vv))
            sv2 = E("s", lambda h: h.activation(af(VAR, TC), af(VAR, TC), AF.Sqrt,
                                                bias=scb[:, 11:12]), "ss")
            W("v", ("ss", sv2))
            def vf2(h, sl=sl, dst=dst):
                h.reciprocal(af(INV, TC), af(VAR, TC))
                last = None
                for c in range(8):
                    h.tensor_tensor(af(M2, TC), res[:, c, sl], af(MEAN, TC), OP.subtract)
                    last = h.tensor_tensor(dst[:, c, sl], af(M2, TC), af(INV, TC), OP.mult)
                return last
            vv2 = E("v", vf2, "sv")
            guard[0] = ("sv", vv2)
            guard[1] = ("sv", vv2)
            if absmax_to is not None:
                def vf3(h, sl=sl, dst=dst, tc=tc):
                    last = None
                    for c in range(8):
                        last = h.tensor_reduce(af(2560 + tc * 8 + c, 1), dst[:, c, sl],
                                               AX.X, OP.max, apply_absolute_value=True)
                    return last
                E("v", vf3, "sv")

    layernorm(xn)
    if stop("ph1"):
        return finish()

    S0, S1, S2, S3 = 0, 8192, 16384, 24576

    def load_wT(src_h, dst_off):
        dv = DMA("g", ab3(S1, 8, 1024), src_h.rearrange("(rc p) d -> p rc d", p=P))
        W("t", ("dg", dv))
        for ec in range(8):
            for grp in range(2):
                bk = 6 + grp
                tb([bk])
                def tf(h, ec=ec, grp=grp, bk=bk):
                    last = None
                    for j in range(4):
                        oc = grp * 4 + j
                        last = h.transpose(pbs[bk - 6][:, j * P:(j + 1) * P],
                                           ab3(S1, 8, 1024)[:, oc, ec * P:(ec + 1) * P],
                                           eye_b[:, :])
                    return last
                tv = E("t", tf, "st")
                W("s", ("st", tv))
                sv = E("s", lambda h, ec=ec, grp=grp, bk=bk, dst_off=dst_off:
                       h.copy(ab3(dst_off, 8, 1024)[:, ec, grp * 512:(grp + 1) * 512],
                              pbs[bk - 6][:, 0:512]), "ss")
                guard[bk] = ("ss", sv)

    bar("g")
    W("g", ("sv", CNT["sv"]), ("ss", CNT["ss"]))
    load_wT(wk_h, S2)
    bar("g")
    W("g", ("ss", CNT["ss"]))
    load_wT(wv_h, S3)

    bar("t")
    for hh in (() if "ph2" in skip else range(2)):
        for mat_off, wt_off, brow, is_k in ((S0, S2, rowA, True), (S1, S3, rowB, False)):
            mat = ab3(mat_off, 8, 1024)
            for tsub in range(8):
                for dh in range(2):
                    bk = (tsub * 2 + dh) % 4
                    tb([bk])
                    def tf(h, hh=hh, tsub=tsub, dh=dh, bk=bk, wt_off=wt_off, brow=brow):
                        t0 = hh * 1024 + tsub * P
                        for kc in range(8):
                            h.matmul(ps[bk][:, :], xn[:, kc, t0:t0 + P],
                                     ab3(wt_off, 8, 1024)[:, kc, dh * 512:(dh + 1) * 512],
                                     start=(kc == 0), stop=False)
                        return h.matmul(ps[bk][:, :], onesr_b[0:1, 0:P],
                                        brow[0:1, dh * 512:(dh + 1) * 512],
                                        start=False, stop=True)
                    tv = E("t", tf, "st")
                    if is_k:
                        W("s", ("st", tv))
                        def sf(h, bk=bk):
                            h.activation(a6(0, TC), ps[bk][:, :], AF.Exp)
                            return h.activation(a6(512, TC), ps[bk][:, :], AF.Relu)
                        sv = E("s", sf, "ss")
                        W("v", ("ss", sv))
                        vv = E("v", lambda h, mat=mat, tsub=tsub, dh=dh:
                               h.scalar_tensor_tensor(mat[:, tsub, dh * 512:(dh + 1) * 512],
                                                      a6(0, TC), 1.0, a6(512, TC),
                                                      OP.min, OP.add), "sv")
                        guard[bk] = ("sv", vv)
                    else:
                        W("s", ("st", tv))
                        sv = E("s", lambda h, mat=mat, tsub=tsub, dh=dh, bk=bk:
                               h.copy(mat[:, tsub, dh * 512:(dh + 1) * 512], ps[bk][:, :]),
                               "ss")
                        guard[bk] = ("ss", sv)
        kt, vt = ab3(S0, 8, 1024), ab3(S1, 8, 1024)
        W("t", ("ss", CNT["ss"]), ("sv", CNT["sv"]))
        last_acc = 0
        for eh in (() if "ph2kv" in skip else range(2)):
            for dcg in ((0, 1, 2, 3, 4, 5), (6, 7)):
                tb(list(range(len(dcg))))
                def tf(h, eh=eh, dcg=dcg):
                    last = None
                    for i, dc in enumerate(dcg):
                        for tsub in range(8):
                            last = h.matmul(ps[i][:, :], kt[:, tsub, dc * P:(dc + 1) * P],
                                            vt[:, tsub, eh * 512:(eh + 1) * 512],
                                            start=(tsub == 0), stop=(tsub == 7))
                    return last
                tv = E("t", tf, "st")
                if "ph2ke" in skip:
                    guard_upd = ("st", tv)
                    for i in range(len(dcg)):
                        guard[i] = guard_upd
                    continue
                W("s", ("st", tv), ("dg", last_acc))
                def vf(h, dcg=dcg):
                    last = None
                    for i, dc in enumerate(dcg):
                        last = h.copy(af3(0, 6, TC)[:, i, :], ps[i][:, :])
                    return last
                vv = E("s", vf, "ss")
                for i in range(len(dcg)):
                    guard[i] = ("ss", vv)
                W("g", ("ss", vv))
                if "ph2kd" in skip:
                    continue
                kw = {} if (hh == 0 or "noacc" in skip) else dict(accum_op=OP.add)
                last_acc = DMA("g", kv_in[:, dcg[0]:dcg[-1] + 1, eh * 512:(eh + 1) * 512],
                               af3(0, len(dcg), TC), **kw)
        if "ph2ks" in skip:
            continue
        tb([5])
        def tfk(h):
            last = None
            for dc in range(8):
                for tsub in range(8):
                    last = h.matmul(ps[5][:, dc:dc + 1], kt[:, tsub, dc * P:(dc + 1) * P],
                                    ones_b[:, 0:1], start=(tsub == 0), stop=(tsub == 7))
            return last
        tv = E("t", tfk, "st")
        W("v", ("st", tv), ("dg", last_acc))
        vv = E("v", lambda h: h.tensor_copy(af(3584, 8), ps[5][:, 0:8]), "sv")
        guard[5] = ("sv", vv)
        W("g", ("sv", vv))
        kw = {} if (hh == 0 or "noacc" in skip) else dict(accum_op=OP.add)
        last_acc = DMA("g", kv_in[:, :, 1024], af(3584, 8), allow_slow_non_contiguous=True, **kw)

    PH2ON = "ph2" not in skip
    if PH2ON:
        if "ph2ar" not in skip:
            cv = E("g", lambda h: h.collective_compute(
                "AllReduce", OP.add, replica_groups=[list(range(NC8))],
                ins=[kv_in.opt()], outs=[kv_out.opt()]), "cc")

        if "ph2w" not in skip:
            W("g", ("sv", CNT["sv"]), ("st", CNT["st"]))
            load_wT(wq_h, S2)
            bar("g")
            W("g", ("ss", CNT["ss"]))
            load_wT(wo_h, S3)
        if "ph2ar" not in skip:
            W("g", ("cc", cv), ("st", CNT["st"]), ("ss", CNT["ss"]))
            DMA("g", ab3(S1, 8, 1024), kv_out[:, :, 0:1024])
            dvkv = DMA("g", ksum_b[:, :], kv_out[:, :, 1024], allow_slow_non_contiguous=True)
        else:
            dvkv = 0
        kvb = ab3(S1, 8, 1024)

        W("g", ("st", CNT["st"]))
        dvbo = DMA("g", rowA[:, :], bo_h[None, :])
        W("t", ("dg", dvkv), ("dg", dvbo), ("ss", CNT["ss"]))
        qt = ab3(S0, 8, TC)
        ctx = ab3(S0 + 4096, 8, TC)
        for tc in (() if "ph2c" in skip else range(NT)):
            sl = slice(tc * TC, (tc + 1) * TC)
            for qc in range(8):
                bk = qc % 4
                tb([bk])
                def tf(h, qc=qc, bk=bk, sl=sl):
                    last = None
                    for kc in range(8):
                        last = h.matmul(ps[bk][:, :], ab3(S2, 8, 1024)[:, kc, qc * P:(qc + 1) * P],
                                        xn[:, kc, sl], start=(kc == 0), stop=(kc == 7))
                    return last
                tv = E("t", tf, "st")
                W("s", ("st", tv))
                def sf(h, qc=qc, bk=bk):
                    h.activation(a6(0, TC), ps[bk][:, :], AF.Exp, bias=bq_c[:, qc:qc + 1])
                    return h.activation(a6(512, TC), ps[bk][:, :], AF.Relu, bias=bq_c[:, qc:qc + 1])
                sv = E("s", sf, "ss")
                W("v", ("ss", sv))
                vv = E("v", lambda h, qc=qc:
                       h.scalar_tensor_tensor(qt[:, qc, :], a6(0, TC), 1.0, a6(512, TC),
                                              OP.min, OP.add), "sv")
                guard[bk] = ("sv", vv)
            if "ph2z" not in skip:
                tb([4, 5])
                W("t", ("sv", CNT["sv"]))
                tv = E("t", lambda h: [h.matmul(ps[4][0:1, :], ksum_b[:, kc:kc + 1], qt[:, kc, :],
                                                start=(kc == 0), stop=(kc == 7))
                                       for kc in range(8)][-1], "st")
                W("v", ("st", tv))
                def vfi(h):
                    with nc.allow_low_precision(reason="invZ bf16 matches tolerance"):
                        return h.reciprocal(invz[0:1, :], ps[4][0:1, :])
                vv = E("v", vfi, "sv")
                guard[4] = ("sv", vv)
                W("t", ("sv", vv))
                tv = E("t", lambda h: h.matmul(ps[5][:, :], ones_b[0:1, :], invz[0:1, :],
                                               start=True, stop=True), "st")
                W("v", ("st", tv))
                def vfq(h):
                    last = None
                    for qc in range(8):
                        last = h.tensor_tensor(qt[:, qc, :], qt[:, qc, :], ps[5][:, :], OP.mult)
                    return last
                vv = E("v", vfq, "sv")
                guard[5] = ("sv", vv)
            for ec in range(8):
                bk = ec % 4
                tb([bk])
                W("t", ("sv", vv))
                def tf(h, ec=ec, bk=bk):
                    last = None
                    for dc in range(8):
                        last = h.matmul(ps[bk][:, :], kvb[:, dc, ec * P:(ec + 1) * P],
                                        qt[:, dc, :], start=(dc == 0), stop=(dc == 7))
                    return last
                tv = E("t", tf, "st")
                W("s", ("st", tv))
                sv = E("s", lambda h, ec=ec, bk=bk: h.copy(ctx[:, ec, :], ps[bk][:, :]), "ss")
                guard[bk] = ("ss", sv)
            for oc in range(8):
                bk = 4 + (oc % 2)
                tb([bk])
                W("t", ("ss", CNT["ss"]))
                def tf(h, oc=oc, bk=bk):
                    for ec in range(8):
                        h.matmul(ps[bk][:, :], ab3(S3, 8, 1024)[:, ec, oc * P:(oc + 1) * P],
                                 ctx[:, ec, :], start=(ec == 0), stop=False)
                    return h.matmul(ps[bk][:, :], rowA[0:1, oc * P:(oc + 1) * P],
                                    onesr_b[0:1, :], start=False, stop=True)
                tv = E("t", tf, "st")
                W("v", ("st", tv))
                vv = E("v", lambda h, oc=oc, bk=bk, sl=sl:
                       h.scalar_tensor_tensor(res[:, oc, sl], ps[bk][:, :], 1.0,
                                              res[:, oc, sl], OP.mult, OP.add), "sv")
                guard[bk] = ("sv", vv)
    if stop("ph2"):
        return finish()

    bar("t", "v", "s", "g", "y")
    g1, g2 = ab3(0, 32, TC), a63(0, 32, TC)
    k0r_p = [ab3(S2, 16, 256), ab3(S2 + 4096, 16, 256)]
    k0d = ab3(S2 + 8192, 8, 256)
    k1b_p = [ab3(S2, 32, P), ab3(S2 + 4096, 32, P)]
    k2b_p = [ab3(S2, 8, 1024), ab3(S2 + 8192, 8, 1024)]
    for q in (() if "ph3" in skip else range(NT)):
        sl = slice(q * TC, (q + 1) * TC)
        dko = {}
        W("g", ("st", CNT["st"]), ("sv", CNT["sv"]))
        for og in range(2):
            dko[og] = DMA("g", k0r_p[og % 2][:, :, :],
                          k0_h[:, og * 256:(og + 1) * 256].rearrange(
                              "(kc p) n -> p kc n", p=P), q="dg%d" % (og % 2))
        for og in range(16):
            W("v", ("dg%d" % (og % 2), dko[og]), ("st", CNT["st"]))
            def vf(h, og=og):
                last = None
                for kc in range(8):
                    last = h.tensor_tensor(k0d[:, kc, :], k0r_p[og % 2][:, kc, :],
                                           k0r_p[og % 2][:, kc + 8, :], OP.subtract)
                return last
            vv = E("v", vf, "sv")
            if og + 2 < 16:
                W("g", ("sv", vv))
                dko[og + 2] = DMA("g", k0r_p[og % 2][:, :, :],
                                  k0_h[:, (og + 2) * 256:(og + 3) * 256].rearrange(
                                      "(kc p) n -> p kc n", p=P), q="dg%d" % (og % 2))
            for oc2 in range(2):
                bk = (og * 2 + oc2) % 4
                tb([bk])
                W("t", ("sv", vv))
                def tf(h, sl=sl, oc2=oc2, bk=bk):
                    last = None
                    for kc in range(8):
                        last = h.matmul(ps[bk][:, :], k0d[:, kc, oc2 * P:(oc2 + 1) * P],
                                        xn[:, kc, sl], start=(kc == 0), stop=(kc == 7))
                    return last
                tv = E("t", tf, "st")
                W("s", ("st", tv))
                sv = E("s", lambda h, og=og, oc2=oc2, bk=bk:
                       h.activation(g1[:, og * 2 + oc2, :], ps[bk][:, :], AF.Relu,
                                    scale=scb[:, 0:1]), "ss")
                guard[bk] = ("ss", sv)
        tb([4])
        W("t", ("ss", CNT["ss"]))
        tv = E("t", lambda h: [h.matmul(ps[4][:, :], ones_b[:, :], g1[:, kc, :],
                                        start=(kc == 0), stop=(kc == 31))
                               for kc in range(32)][-1], "st")
        W("v", ("st", tv))
        vv_rs = E("v", lambda h: h.tensor_scalar(af(0, TC), ps[4][:, :], scb[:, 2:3],
                                                 None, OP.mult), "sv")
        guard[4] = ("sv", vv_rs)
        dk1 = {}
        W("g", ("st", CNT["st"]), ("sv", CNT["sv"]))
        for og in range(2):
            dk1[og] = DMA("g", k1b_p[og % 2][:, :, :],
                          k1_h[:, og * P:(og + 1) * P].rearrange(
                              "(kc p) n -> p kc n", p=P), q="dg%d" % (og % 2))
        for og in range(32):
            bk = og % 4
            tb([bk])
            W("t", ("dg%d" % (og % 2), dk1[og]))
            def tf(h, og=og, bk=bk):
                last = None
                for kc in range(32):
                    last = h.matmul(ps[bk][:, :], k1b_p[og % 2][:, kc, :],
                                    g1[:, kc, :], start=(kc == 0), stop=(kc == 31))
                return last
            tv = E("t", tf, "st")
            if og + 2 < 32:
                W("g", ("st", tv))
                dk1[og + 2] = DMA("g", k1b_p[og % 2][:, :, :],
                                  k1_h[:, (og + 2) * P:(og + 3) * P].rearrange(
                                      "(kc p) n -> p kc n", p=P), q="dg%d" % (og % 2))
            W("v", ("st", tv), ("sv", vv_rs))
            vv = E("v", lambda h, bk=bk:
                   h.scalar_tensor_tensor(af(2048, TC), ps[bk][:, :], scb[:, 1:2],
                                          af(0, TC), OP.mult, OP.subtract), "sv")
            W("s", ("sv", vv))
            sv = E("s", lambda h, og=og:
                   h.activation(g2[:, og, :], af(2048, TC), AF.Relu, scale=2.0), "ss")
            guard[bk] = ("ss", sv)
        tb([4])
        W("t", ("ss", CNT["ss"]))
        tv = E("t", lambda h: [h.matmul(ps[4][:, :], ones_b[:, :], g2[:, kc, :],
                                        start=(kc == 0), stop=(kc == 31))
                               for kc in range(32)][-1], "st")
        W("v", ("st", tv))
        vv_rs2 = E("v", lambda h: h.tensor_scalar(af(512, TC), ps[4][:, :], scb[:, 4:5],
                                                  None, OP.mult), "sv")
        guard[4] = ("sv", vv_rs2)
        dk2 = {}
        W("g", ("st", CNT["st"]))
        for kq in range(2):
            dk2[kq] = DMA("g", k2b_p[kq % 2][:, :, :],
                          k2_h[kq * 1024:(kq + 1) * 1024, :].rearrange(
                              "(kc p) n -> p kc n", p=P), q="dg%d" % (kq % 2))
        for ocg in range(2):
            for kq in range(4):
                if not (ocg == 0 and kq < 2):
                    W("g", ("st", CNT["st"]))
                    dk2[kq] = DMA("g", k2b_p[kq % 2][:, :, :],
                                  k2_h[kq * 1024:(kq + 1) * 1024, :].rearrange(
                                      "(kc p) n -> p kc n", p=P), q="dg%d" % (kq % 2))
                for oc4 in range(4):
                    oc = ocg * 4 + oc4
                    bk = [0, 1, 4, 5][oc4]
                    if kq == 0:
                        tb([bk])
                    W("t", ("dg%d" % (kq % 2), dk2[kq]))
                    def tf(h, kq=kq, oc=oc, bk=bk):
                        last = None
                        for kc in range(8):
                            last = h.matmul(ps[bk][:, :], k2b_p[kq % 2][:, kc, oc * P:(oc + 1) * P],
                                            g2[:, kq * 8 + kc, :],
                                            start=(kq == 0 and kc == 0),
                                            stop=(kq == 3 and kc == 7))
                        return last
                    tv = E("t", tf, "st")
                    if kq == 3:
                        W("v", ("st", tv), ("sv", vv_rs2))
                        def vf(h, oc=oc, bk=bk, sl=sl):
                            h.scalar_tensor_tensor(af(2048, TC), ps[bk][:, :], scb[:, 3:4],
                                                   af(512, TC), OP.mult, OP.subtract)
                            return h.tensor_tensor(res[:, oc, sl], res[:, oc, sl],
                                                   af(2048, TC), OP.add)
                        vv = E("v", vf, "sv")
                        guard[bk] = ("sv", vv)
    if stop("ph3"):
        return finish()

    bar("t", "v", "s", "g", "y")
    layernorm(xn, absmax_to=xm_col[:, 0:1])
    def vvdf(h):
        h.tensor_copy(AFr[:, 3200:3232], AFr[:, 2560:2592])
        h.tensor_copy(AFr[:, 3200:3232], AFr[:, 2560:2592])
        return h.tensor_scalar(af(5950, 32), af(2560, 32), 1.0, 0.0,
                               OP.mult, OP.max, accum_out=xden[:, :])
    vvd = E("v", vvdf, "sv")
    if stop_after == "ph4":
        def vdbg(h):
            h.tensor_copy(AFr[:, 3072:3104], af(2560, 32))
            return h.tensor_copy(AFr[:, 3104:3105], xden[:, :])
        vq = E("v", vdbg, "sv")
        W("y", ("sv", vq))
        DMA("y", dbg["cols"], AFr[:, 3072:3105])
    W("g", ("sv", CNT["sv"]))
    gv = E("g", lambda h: h.tensor_reduce(invz[0:1, 0:1], xden[:, :], AX.C, OP.max),
           "sg")
    tb([5])
    W("t", ("sg", gv))
    tv = E("t", lambda h: h.matmul(ps[5][:, 0:1], ones_b[0:1, :], invz[0:1, 0:1],
                                   start=True, stop=True), "st")
    W("v", ("st", tv))
    vv = E("v", lambda h: h.tensor_copy(xm_col[:, 0:1], ps[5][:, 0:1]), "sv")
    guard[5] = ("sv", vv)
    W("y", ("sv", CNT["sv"]))
    DMA("y", xn1_sp[:, :, :], xn[:, :, :])

    def quant_weight(src_h, nrc, L, dst_sp, sc_dst):
        last_dma = 0
        for ic in range(nrc):
            W("g", ("sv", CNT["sv"]))
            dv = DMA("g", af(0, L), src_h[ic * P:(ic + 1) * P, :])
            W("v", ("dg", dv), ("dy", last_dma))
            def vf(h, ic=ic, sc_dst=sc_dst, L=L):
                h.tensor_reduce(af(5888, 1), af(0, L), AX.X, OP.max,
                                apply_absolute_value=True)
                h.tensor_copy(af(5952, 64), af(0, 64))
                h.tensor_copy(af(5952, 64), af(0, 64))
                h.tensor_scalar(sc_dst[:, ic:ic + 1], af(5888, 1), 1.0 / 127.0,
                                None, OP.mult)
                h.reciprocal(af(5890, 1), af(5888, 1))
                h.tensor_copy(af(5952, 64), af(0, 64))
                h.tensor_copy(af(5952, 64), af(0, 64))
                h.tensor_scalar(af(3072, L), af(0, L), af(5890, 1), 127.0,
                                OP.mult, OP.mult)
                return h.tensor_scalar(ab(0, L), af(3072, L), RC, -RC,
                                       OP.add, OP.add)
            vv = E("v", vf, "sv")
            W("y", ("sv", vv))
            last_dma = DMA("y", dst_sp[ic * P:(ic + 1) * P, :], ab(0, L))
    quant_weight(wg_h, IHC, DM, wgq_sp, sg_c)
    quant_weight(wu_h, IHC, DM, wuq_sp, su_c)
    quant_weight(wd_h, DC, IH, wdq_sp, sd_c)
    if stop("ph4"):
        return finish()

    bar("t", "v", "s", "g", "y")
    dv_wi = DMA("g", ab3(0, 8, DI), win_h[:, 0:DI].rearrange("(kc p) m -> p kc m", p=P))
    W("t", ("dg", dv_wi), ("sv", CNT["sv"]))
    xhpq_a = ab3(S2, 16, 516)
    last_spill = 0
    for q in range(NT):
        sl = slice(q * TC, (q + 1) * TC)
        for m in range(DIC):
            bk = m % 4
            tb([bk])
            def tf(h, sl=sl, m=m, bk=bk):
                last = None
                for kc in range(8):
                    last = h.matmul(ps[bk][:, :], ab3(0, 8, DI)[:, kc, m * P:(m + 1) * P],
                                    xn[:, kc, sl], start=(kc == 0), stop=(kc == 7))
                return last
            tv = E("t", tf, "st")
            W("s", ("st", tv), ("dy", last_spill))
            sv = E("s", lambda h, m=m, bk=bk:
                   h.activation(xhpq_a[:, m, 3:515], ps[bk][:, :], AF.Identity,
                                bias=binxh_c[:, m:m + 1]), "ss")
            guard[bk] = ("ss", sv)
        W("y", ("ss", CNT["ss"]))
        last_spill = DMA("y", xhp_sp[:, :, sl], xhpq_a[:, :, 3:515])
    W("v", ("ss", CNT["ss"]), ("sg", CNT["sg"]))
    def vagp(h):
        h.tensor_copy(af3(0, 16, 3), xhpq_a[:, :, 512:515])
        return h.tensor_copy(af(48, 1), xm_col[:, 0:1])
    vv = E("v", vagp, "sv")
    W("y", ("sv", vv))
    dvh = DMA("y", ag1_in[:, 0:49], af(0, 49))
    W("g", ("dy", dvh))
    cv = E("g", lambda h: h.collective_compute(
        "AllGather", OP.bypass, replica_groups=[list(range(NC8))],
        ins=[ag1_in.opt()], outs=[ag1_out.opt()]), "cc")
    W("g", ("cc", cv))
    dvh2 = DMA("g", af3(1024, 8, 52), ag1_out[:, :, :].rearrange("j p f -> p j f"))
    W("v", ("dg", dvh2))
    def vsel(h):
        agt = af3(1024, 8, 52)
        h.tensor_copy(af(2100, 8), agt[:, :, 48])
        h.memset(af(2052, 48), 0.0)
        for j in range(4):
            h.scalar_tensor_tensor(af(2052, 48), agt[:, j, 0:48], msk_c[:, j:j + 1],
                                   af(2052, 48), OP.mult, OP.add)
        h.tensor_reduce(xm_col[:, 2:3], af(2100, 8), AX.X, OP.max)
        for j in range(4, 8):
            h.scalar_tensor_tensor(af(2052, 48), agt[:, j, 0:48], msk_c[:, j:j + 1],
                                   af(2052, 48), OP.mult, OP.add)
        h.tensor_scalar(af(2048, 1), scb[:, 5:6], xm_col[:, 2:3], None, OP.mult)
        h.tensor_copy(af(2108, 8), agt[:, :, 48])
        h.tensor_copy(af(2108, 8), agt[:, :, 48])
        h.reciprocal(af(2049, 1), af(2048, 1))
        h.tensor_copy(af(2108, 8), agt[:, :, 48])
        h.tensor_copy(af(2108, 8), agt[:, :, 48])
        h.tensor_scalar(scb[:, 7:8], af(2049, 1), 127.0, None, OP.mult)
        h.tensor_scalar(scb[:, 8:9], af(2048, 1), 1.0 / 127.0, None, OP.mult)
        h.tensor_copy(af(2108, 8), agt[:, :, 48])
        h.tensor_copy(af(2108, 8), agt[:, :, 48])
        h.tensor_scalar(gsc_c[:, :], sg_c[:, :], scb[:, 8:9], None, OP.mult)
        return h.tensor_scalar(usc_c[:, :], su_c[:, :], scb[:, 8:9], None, OP.mult)
    vv_sel = E("v", vsel, "sv")

    bar("t", "v", "s", "g", "y")
    xhpq = ab3(0, 16, 516)
    xhq = ab3(8256, 16, TC)
    duq = ab3(16448, 16, TC)
    taps = ab3(24640, 4, TC)
    dlq = a63(0, 16, TC)
    wdt_p = [ab3(26688, 16, P), ab3(26688 + 2048, 16, P)]
    last_tail = 0
    last_q_dmas = 0
    for q in range(NT):
        sl = slice(q * TC, (q + 1) * TC)
        W("y", ("sg", last_tail), ("ss", CNT["ss"]), ("sv", CNT["sv"]))
        dv = DMA("y", xhpq[:, :, 3:515], xhp_sp[:, :, sl])
        if q == 0:
            W("v", ("dy", dv), ("sv", vv_sel))
            vv0 = E("v", lambda h: h.tensor_copy(xhpq[:, :, 0:3], af3(2052, 16, 3)), "sv")
        else:
            W("v", ("dy", dv), ("sg", last_tail))
            vv0 = E("v", lambda h: h.tensor_copy(xhpq[:, :, 0:3], tail_sb[:, :, :]), "sv")
        W("g", ("dy", dv), ("sv", vv0))
        last_tail = E("g", lambda h: h.tensor_copy(tail_sb[:, :, :], xhpq[:, :, 512:515]),
                      "sg")
        for db in range(DIC):
            def sf(h, db=db):
                last = None
                for k in range(4):
                    last = h.activation(taps[:, k, :], xhpq[:, db, k:k + TC], AF.Identity,
                                        scale=cw_c[:, db, k:k + 1])
                return last
            W("s", ("sv", vv0 if db == 0 else CNT["sv"]))
            sv = E("s", sf, "ss")
            W("v", ("ss", sv))
            def vf(h):
                h.tensor_tensor(taps[:, 0, :], taps[:, 0, :], taps[:, 1, :], OP.add)
                h.tensor_tensor(taps[:, 2, :], taps[:, 2, :], taps[:, 3, :], OP.add)
                return h.tensor_tensor(taps[:, 0, :], taps[:, 0, :], taps[:, 2, :], OP.add)
            vv = E("v", vf, "sv")
            W("s", ("sv", vv))
            E("s", lambda h, db=db: h.activation(xhq[:, db, :], taps[:, 0, :], AF.Silu,
                                                 bias=cb_c[:, db:db + 1]), "ss")
        W("y", ("ss", CNT["ss"]))
        DMA("y", xh_sp[:, :, sl], xhq[:, :, :])
        tb([4])
        W("t", ("ss", CNT["ss"]))
        tv = E("t", lambda h: [h.matmul(ps[4][0:DS, :], wx_sb[:, db, :], xhq[:, db, :],
                                        start=(db == 0), stop=(db == DIC - 1))
                               for db in range(DIC)][-1], "st")
        W("s", ("st", tv))
        sv = E("s", lambda h, sl=sl: h.activation(B_sb[:, sl], ps[4][0:DS, :], AF.Identity,
                                                  bias=bx_c[:, :]), "ss")
        guard[4] = ("ss", sv)
        dvm = {}
        for m in range(2):
            dvm[m] = DMA("g", wdt_p[m % 2][:, :, :],
                         wdt_h[:, m * P:(m + 1) * P].rearrange("(kc p) n -> p kc n", p=P),
                         q="dg%d" % (m % 2))
        for m in range(DIC):
            bk = m % 4
            tb([bk])
            W("t", ("dg%d" % (m % 2), dvm[m]))
            def tf(h, m=m, bk=bk):
                last = None
                for kc in range(DIC):
                    last = h.matmul(ps[bk][:, :], wdt_p[m % 2][:, kc, :], xhq[:, kc, :],
                                    start=(kc == 0), stop=(kc == DIC - 1))
                return last
            tv = E("t", tf, "st")
            if m + 2 < DIC:
                W("g", ("st", tv))
                dvm[m + 2] = DMA("g", wdt_p[m % 2][:, :, :],
                                 wdt_h[:, (m + 2) * P:(m + 3) * P].rearrange(
                                     "(kc p) n -> p kc n", p=P), q="dg%d" % (m % 2))
            W("s", ("st", tv))
            def sf(h, m=m, bk=bk):
                h.activation(af(2048, TC), ps[bk][:, :], AF.Exp, bias=bdt_c[:, m:m + 1])
                return h.activation(dlq[:, m, :], af(2048, TC), AF.Ln, bias=scb[:, 12:13])
            sv = E("s", sf, "ss")
            guard[bk] = ("ss", sv)
            W("v", ("ss", sv))
            E("v", lambda h, m=m: h.tensor_tensor(duq[:, m, :], dlq[:, m, :],
                                                  xhq[:, m, :], OP.mult), "sv")
        W("y", ("ss", CNT["ss"]), ("sv", CNT["sv"]))
        DMA("y", dl_sp[:, :, sl], dlq[:, :, :])
        last_q_dmas = DMA("y", du_sp[:, :, sl], duq[:, :, :])
    if stop("ph5"):
        return finish()

    bar("t", "v", "s", "g", "y")
    bb8 = ab3(0, 8, T)
    dudb = ab3(16384, 2, T)
    hout = ab3(20480, 2, T)
    xhdb = ab3(24576, 2, T)
    ddb = a63(0, 2, T)
    apng = a63(4096, 2, T)
    y16 = a63(8192, 2, T)
    d16 = a63(12288, 2, T)

    sel_all = AB[0:DS, 30720:32768]
    gv = E("g", lambda h: (
        h.iota(AFr[0:DS, 0:2048], [[0, 2048]], channel_multiplier=1,
               allow_small_or_imprecise_dtypes=True),
        h.iota(AFr[0:DS, 2048:4096], [[1, DS], [0, P]], channel_multiplier=0,
               allow_small_or_imprecise_dtypes=True))[1], "sg")
    W("v", ("sg", gv))
    vsel0 = E("v", lambda h: h.tensor_tensor(sel_all, AFr[0:DS, 0:2048],
                                             AFr[0:DS, 2048:4096], OP.is_equal), "sv")
    last_iter_sv = {}
    hend_sg = {}
    for sgp in range(2):
        for si in range(8):
            s = sgp * 8 + si
            for tcb in range(NT):
                bk = 4 + (tcb % 2)
                tb([bk])
                W("t", ("sv", vsel0))
                def tf(h, s=s, tcb=tcb, bk=bk):
                    return h.matmul(ps[bk][:, :], AB[0:DS, 30720 + s * P:30720 + (s + 1) * P],
                                    B_sb[:, tcb * TC:(tcb + 1) * TC],
                                    start=True, stop=True)
                tv = E("t", tf, "st")
                W("s", ("st", tv), ("sv", CNT["sv"]))
                sv = E("s", lambda h, si=si, tcb=tcb, bk=bk:
                       h.copy(bb8[:, si, tcb * TC:(tcb + 1) * TC], ps[bk][:, :]), "ss")
                guard[bk] = ("ss", sv)
        bar("v")
        for db in range(DIC):
            pp = db % 2
            qd = "y" if pp == 0 else "s"
            qsem = QS[qd]
            gate = last_iter_sv.get((sgp, db - 2)) or (last_iter_sv.get((sgp - 1, DIC - 2 + db)) if sgp > 0 else None)
            if gate:
                W(qd, ("sv", gate))
            d1 = DMA(qd, ddb[:, pp, :], dl_sp[:, db, :])
            d2 = DMA(qd, dudb[:, pp, :], du_sp[:, db, :])
            d3 = None
            if sgp == 0:
                d3 = DMA(qd, xhdb[:, pp, :], xh_sp[:, db, :])
            else:
                d3 = DMA(qd, y16[:, pp, :], y_sp[:, db, :])
            dlast = d3
            W("v", (qsem, dlast))
            if sgp == 0:
                def vf(h, db=db, pp=pp):
                    h.tensor_tensor_scan(af(4096, T), ones_b[:, 0:1].broadcast_to([P, T]),
                                         ddb[:, pp, :], 0.0, OP.mult, OP.add)
                    return h.tensor_scalar(af(pp * T, T), xhdb[:, pp, :],
                                           dvec_c[:, db:db + 1], None, OP.mult)
                vcum = E("v", vf, "sv")
                vcv = E("v", lambda h, pp=pp: h.tensor_copy(d16[:, pp, :], af(4096, T)),
                        "sv")
                W(qd, ("sv", vcv))
                DMA(qd, dc_sp[:, db, :], d16[:, pp, :])
                def vpre(h, db=db):
                    for qq in range(1, 4):
                        h.tensor_scalar(dpre_sb[:, qq, db * DS:(db + 1) * DS],
                                        msk_c[:, 0:DS], 0.0, af(4096 + qq * TC - 1, 1),
                                        OP.mult, OP.add)
                    return h.tensor_scalar(dtotT[:, db * DS:(db + 1) * DS],
                                           msk_c[:, 0:DS], 0.0, af(4096 + T - 1, 1),
                                           OP.mult, OP.add)
                E("v", vpre, "sv")
            else:
                vcum = E("v", lambda h, db=db, pp=pp:
                         h.tensor_copy(af(pp * T, T), y16[:, pp, :]), "sv")
            for si in range(8):
                s = sgp * 8 + si
                prevv = last_iter_sv.get((sgp, db))
                W("s", (qsem, dlast))
                if (sgp, db, si - 2) in last_iter_sv or True:
                    pass
                gate_a = None
                if si >= 2:
                    gate_a = last_iter_sv[(sgp, db, si - 2)]
                elif db >= 1:
                    gate_a = last_iter_sv.get((sgp, db - 1, 6 + si))
                elif sgp == 1:
                    gate_a = last_iter_sv.get((0, DIC - 1, 6 + si))
                if gate_a:
                    W("s", ("sv", gate_a))
                sva = E("s", lambda h, db=db, s=s, si=si, pp=pp:
                        h.activation(apng[:, si % 2, :], ddb[:, pp, :], AF.Exp,
                                     scale=A_sl[:, db * DS + s:db * DS + s + 1]), "ss")
                W("v", ("ss", sva))
                gate_h = None
                if si >= 2:
                    gate_h = hend_sg.get((sgp, db, si - 2))
                elif db >= 1:
                    gate_h = hend_sg.get((sgp, db - 1, 6 + si))
                elif sgp == 1:
                    gate_h = hend_sg.get((0, DIC - 1, 6 + si))
                if gate_h:
                    W("v", ("sg", gate_h))
                def vf2(h, db=db, s=s, si=si, pp=pp):
                    h.tensor_tensor(ab(28672, T), dudb[:, pp, :], bb8[:, si, :], OP.mult)
                    return h.tensor_tensor_scan(hout[:, si % 2, :], apng[:, si % 2, :],
                                                ab(28672, T), 0.0, OP.mult, OP.add)
                vv = E("v", vf2, "sv")
                last_iter_sv[(sgp, db, si)] = vv
                W("g", ("sv", vv))
                def gf2(h, db=db, s=s, si=si, pp=pp):
                    h.tensor_tensor(af(pp * T, T), af(pp * T, T),
                                    hout[:, si % 2, :], OP.add)
                    return h.tensor_copy(hend_sb[:, db * DS + s:db * DS + s + 1],
                                         hout[:, si % 2, T - 1:T])
                gv = E("g", gf2, "sg")
                hend_sg[(sgp, db, si)] = gv
            last_iter_sv[(sgp, db)] = last_iter_sv[(sgp, db, 7)]
            W("v", ("sg", hend_sg[(sgp, db, 7)]))
            vcv = E("v", lambda h, pp=pp: h.tensor_copy(y16[:, pp, :], af(pp * T, T)),
                    "sv")
            W(qd, ("sv", vcv))
            DMA(qd, y_sp[:, db, :], y16[:, pp, :])
            last_iter_sv[(sgp, db)] = vcv
    bar("v")
    vv = E("v", lambda h: h.tensor_tensor(af(0, 256), A_sl[:, :], dtotT[:, :],
                                          OP.mult), "sv")
    W("s", ("sv", vv))
    E("s", lambda h: h.activation(gam_sb[:, :], af(0, 256), AF.Exp), "ss")
    if stop("ph6"):
        return finish()

    bar("t", "v", "s", "g", "y")
    xq = ab3(0, 8, TC)
    gsb = a63(0, IHC, TC)
    wgt = ab3(4096, 8, IH)
    hspill = 0
    for q in range(NT):
        sl = slice(q * TC, (q + 1) * TC)
        W("v", ("sg", CNT["sg"]), ("st", CNT["st"]))
        def vq(h, sl=sl):
            last = None
            for c in range(8):
                h.tensor_scalar(af(0, TC), xn[:, c, sl], scb[:, 7:8], RC, OP.mult, OP.add)
                h.tensor_scalar(af(512, TC), af(0, TC), -RC, 127.0, OP.add, OP.min)
                last = h.tensor_scalar(xq[:, c, :], af(512, TC), -128.0, None, OP.max)
            return last
        vv_xq = E("v", vq, "sv")
        for wsp, scc, is_gate in ((wgq_sp, gsc_c, True), (wuq_sp, usc_c, False)):
            W("s", ("st", CNT["st"]))
            dvw = 0
            for hc in range(8):
                dvw = DMA("s", wgt[:, hc, :], wsp[:, hc * P:(hc + 1) * P], transpose=True)
            for ic in range(IHC):
                bk = ic % 4
                tb([bk])
                W("t", ("da", dvw), ("sv", vv_xq))
                def tf(h, ic=ic, bk=bk):
                    last = None
                    for hc in range(8):
                        last = h.matmul(ps[bk][:, :], wgt[:, hc, ic * P:(ic + 1) * P],
                                        xq[:, hc, :], start=(hc == 0), stop=(hc == 7))
                    return last
                tv = E("t", tf, "st")
                if is_gate:
                    W("s", ("st", tv))
                    sv = E("s", lambda h, ic=ic, bk=bk:
                           h.activation(gsb[:, ic, :], ps[bk][:, :], AF.Silu,
                                        scale=gsc_c[:, ic:ic + 1]), "ss")
                    guard[bk] = ("ss", sv)
                else:
                    W("v", ("st", tv), ("ss", CNT["ss"]), ("dy", hspill))
                    hfo = 2048 + (ic % 4) * TC
                    def vf(h, q=q, ic=ic, bk=bk, hfo=hfo):
                        h.scalar_tensor_tensor(af(hfo, TC), ps[bk][:, :],
                                               usc_c[:, ic:ic + 1], gsb[:, ic, :],
                                               OP.mult, OP.mult)
                        return h.tensor_reduce(af(4704 + q * 22 + ic, 1), af(hfo, TC),
                                               AX.X, OP.max, apply_absolute_value=True)
                    vv = E("v", vf, "sv")
                    guard[bk] = ("sv", vv)
                    if ic % 4 == 3 or ic == IHC - 1:
                        W("y", ("sv", vv))
                        n = (ic % 4) + 1
                        hspill = DMA("y", h_sp[:, ic - n + 1:ic + 1, sl],
                                     af3(2048, n, TC))
    def vvd7(h):
        h.tensor_copy(AFr[:, 4800:4888], AFr[:, 4704:4792])
        h.tensor_copy(AFr[:, 4800:4888], AFr[:, 4704:4792])
        return h.tensor_reduce(xden[:, :], af(4704, 88), AX.X, OP.max)
    vvd = E("v", vvd7, "sv")
    W("g", ("sv", CNT["sv"]))
    gv = E("g", lambda h: h.tensor_reduce(invz[0:1, 0:1], xden[:, :], AX.C, OP.max),
           "sg")
    tb([5])
    W("t", ("sg", gv))
    tv = E("t", lambda h: h.matmul(ps[5][:, 0:1], ones_b[0:1, :], invz[0:1, 0:1],
                                   start=True, stop=True), "st")
    W("v", ("st", tv))
    vv = E("v", lambda h: h.tensor_copy(xm_col[:, 1:2], ps[5][:, 0:1]), "sv")
    guard[5] = ("sv", vv)
    if stop("ph7"):
        return finish()

    bar("t", "v", "s", "g", "y")
    def vpack(h):
        h.tensor_copy(af(0, 256), gam_sb[:, :])
        h.tensor_copy(af(256, 256), hend_sb[:, :])
        return h.tensor_copy(af(512, 1), xm_col[:, 1:2])
    vv = E("v", vpack, "sv")
    W("y", ("sv", vv))
    dv = DMA("y", ag2_in[:, 0:513], af(0, 513))
    W("g", ("dy", dv))
    cv = E("g", lambda h: h.collective_compute(
        "AllGather", OP.bypass, replica_groups=[list(range(NC8))],
        ins=[ag2_in.opt()], outs=[ag2_out.opt()]), "cc")
    W("g", ("cc", cv))
    dv2 = DMA("g", af3(1024, 8, 516), ag2_out[:, :, :].rearrange("j p f -> p j f"))
    W("v", ("dg", dv2), ("sg", CNT["sg"]))
    SUF, TMP, TM2 = 5152, 5408, 5664
    def vcomb(h):
        agt = af3(1024, 8, 516)
        h.memset(af(SUF, 256), 1.0)
        h.memset(hin_sb[:, :], 0.0)
        for j in range(6, -1, -1):
            h.tensor_scalar(af(TMP, 256), agt[:, j + 1, 0:256], msk_c[:, 8 + j + 1:10 + j],
                            omsk_c[:, 8 + j + 1:10 + j], OP.mult, OP.add)
            h.tensor_tensor(af(SUF, 256), af(SUF, 256), af(TMP, 256), OP.mult)
            h.tensor_tensor(af(TM2, 256), agt[:, j, 256:512], af(SUF, 256), OP.mult)
            h.scalar_tensor_tensor(hin_sb[:, :], af(TM2, 256), msk_c[:, 8 + j:9 + j],
                                   hin_sb[:, :], OP.mult, OP.add)
        h.tensor_copy(af(5920, 8), agt[:, :, 512])
        h.tensor_copy(af(5940, 8), agt[:, :, 512])
        h.tensor_copy(af(5940, 8), agt[:, :, 512])
        h.tensor_reduce(xm_col[:, 3:4], af(5920, 8), AX.X, OP.max)
        h.tensor_copy(af(5940, 8), agt[:, :, 512])
        h.tensor_copy(af(5940, 8), agt[:, :, 512])
        h.tensor_scalar(af(TMP, 1), scb[:, 6:7], xm_col[:, 3:4], None, OP.mult)
        h.tensor_copy(af(5940, 8), agt[:, :, 512])
        h.tensor_copy(af(5940, 8), agt[:, :, 512])
        h.reciprocal(af(TMP + 1, 1), af(TMP, 1))
        h.tensor_copy(af(5940, 8), agt[:, :, 512])
        h.tensor_copy(af(5940, 8), agt[:, :, 512])
        h.tensor_scalar(scb[:, 9:10], af(TMP + 1, 1), 127.0, None, OP.mult)
        h.tensor_scalar(scb[:, 10:11], af(TMP, 1), 1.0 / 127.0, None, OP.mult)
        h.tensor_copy(af(5940, 8), agt[:, :, 512])
        h.tensor_copy(af(5940, 8), agt[:, :, 512])
        return h.tensor_scalar(dsc_c[:, :], sd_c[:, :], scb[:, 10:11], None, OP.mult)
    vv2 = E("v", vcomb, "sv")
    if stop("ph8"):
        return finish()

    bar("t", "v", "s", "g", "y")
    dvbt = DMA("g", rowB[:, :], bout_h[None, :])
    yqa = a63(0, 16, TC)
    dqa = a63(8192, 16, TC)
    corrp = ab3(0, 2, TC)
    ygall = ab3(2048, 16, TC)
    zdbuf = ab(10240, TC)
    winz_p = [ab3(10752, 8, P), ab3(11776, 8, P)]
    woutb = ab3(12800, DIC, DM)
    hq = ab3(0, IHC, TC)
    wdh = ab3(11264, 11, 1024)
    for q in range(NT):
        sl = slice(q * TC, (q + 1) * TC)
        bar("y", "s", "g")
        d1 = DMA("y", yqa[:, :, :], y_sp[:, :, sl])
        d2 = DMA("y", dqa[:, :, :], dc_sp[:, :, sl])
        dvo = DMA("g", woutb[:, :, :], wout_h.rearrange("(kc p) m -> p kc m", p=P))
        if q == 0:
            E("v", lambda h: h.tensor_copy(hinq1[:, :], hin_sb[:, :]), "sv")
        else:
            vq3 = E("v", lambda h, q=q: h.tensor_tensor(af(1024, 256), A_sl[:, :],
                                                        dpre_sb[:, q, :], OP.mult), "sv")
            W("s", ("sv", vq3))
            svq = E("s", lambda h: h.activation(af(1280, 256), af(1024, 256), AF.Exp),
                    "ss")
            W("v", ("ss", svq))
            E("v", lambda h: h.tensor_tensor(hinq1[:, :], hin_sb[:, :],
                                             af(1280, 256), OP.mult), "sv")
        corrE = [ab3(0, 8, TC), ab3(8192, 8, TC)]
        corr_sv = {}
        W("s", ("dy", d2), ("sv", CNT["sv"]))
        for db in range(DIC):
            for sg in range(2):
                pp9 = (db * 2 + sg) % 2
                gate_a = corr_sv.get((db, sg - 2)) or corr_sv.get((db - 1, sg))
                if gate_a:
                    W("s", ("sv", gate_a))
                def sf9(h, db=db, sg=sg, pp9=pp9):
                    last = None
                    for si in range(8):
                        s = sg * 8 + si
                        last = h.activation(corrE[pp9][:, si, :], dqa[:, db, :], AF.Exp,
                                            scale=A_sl[:, db * DS + s:db * DS + s + 1])
                    return last
                sva = E("s", sf9, "ss")
                W("v", ("ss", sva), ("dy", d2))
                def vf9(h, db=db, sg=sg, pp9=pp9):
                    last = None
                    for si in range(8):
                        s = sg * 8 + si
                        last = h.scalar_tensor_tensor(yqa[:, db, :], corrE[pp9][:, si, :],
                                                      hinq1[:, db * DS + s:db * DS + s + 1],
                                                      yqa[:, db, :], OP.mult, OP.add)
                    return last
                vv = E("v", vf9, "sv")
                corr_sv[(db, sg)] = vv
        dvz = {}
        for db in range(2):
            dvz[db] = DMA("g", winz_p[db % 2][:, :, :],
                          win_h[:, DI + db * P:DI + (db + 1) * P].rearrange(
                              "(kc p) n -> p kc n", p=P), q="dg%d" % (db % 2))
        for db in range(DIC):
            bk = db % 2
            tb([bk])
            W("t", ("dg%d" % (db % 2), dvz[db]), ("sv", CNT["sv"]))
            def tf(h, db=db, bk=bk, sl=sl):
                last = None
                for kc in range(8):
                    last = h.matmul(ps[bk][:, :], winz_p[db % 2][:, kc, :],
                                    xn[:, kc, sl], start=(kc == 0), stop=(kc == 7))
                return last
            tv = E("t", tf, "st")
            if db + 2 < DIC:
                W("g", ("st", tv))
                dvz[db + 2] = DMA("g", winz_p[db % 2][:, :, :],
                                  win_h[:, DI + (db + 2) * P:DI + (db + 3) * P].rearrange(
                                      "(kc p) n -> p kc n", p=P), q="dg%d" % (db % 2))
            W("s", ("st", tv), ("sv", CNT["sv"]))
            sv = E("s", lambda h, db=db, bk=bk:
                   h.activation(zdbuf[:, :], ps[bk][:, :], AF.Silu,
                                bias=binz_c[:, db:db + 1]), "ss")
            guard[bk] = ("ss", sv)
            W("v", ("ss", sv))
            E("v", lambda h, db=db: h.tensor_tensor(ygall[:, db, :], yqa[:, db, :],
                                                    zdbuf[:, :], OP.mult), "sv")
        W("t", ("dg", dvo), ("dg", dvbt), ("sv", CNT["sv"]))
        for half in range(2):
            tb([0, 1, 2, 3])
            def tf(h, half=half):
                for db in range(DIC):
                    for j in range(4):
                        oc = half * 4 + j
                        h.matmul(ps[j][:, :], woutb[:, db, oc * P:(oc + 1) * P],
                                 ygall[:, db, :], start=(db == 0), stop=False)
                last = None
                for j in range(4):
                    oc = half * 4 + j
                    last = h.matmul(ps[j][:, :], rowB[0:1, oc * P:(oc + 1) * P],
                                    onesr_b[0:1, :], start=False, stop=True)
                return last
            tv = E("t", tf, "st")
            W("v", ("st", tv))
            def vf(h, half=half, sl=sl):
                last = None
                for j in range(4):
                    oc = half * 4 + j
                    last = h.scalar_tensor_tensor(res[:, oc, sl], ps[j][:, :], 1.0,
                                                  res[:, oc, sl], OP.mult, OP.add)
                return last
            vv = E("v", vf, "sv")
            for j in range(4):
                guard[j] = ("sv", vv)
        d4last = 0
        for grp, ics in enumerate(tuple((i, min(i + 2, IHC)) for i in range(0, IHC, 2))):
            lo, hi = ics
            W("y", ("sv", CNT["sv"]))
            d4 = DMA("y", af3(0, hi - lo, TC), h_sp[:, lo:hi, sl])
            W("v", ("dy", d4), ("sv", CNT["sv"]))
            def vf(h, lo=lo, hi=hi):
                last = None
                for i in range(hi - lo):
                    h.tensor_scalar(af(1024, TC), af3(0, hi - lo, TC)[:, i, :],
                                    scb[:, 9:10], RC, OP.mult, OP.add)
                    h.tensor_scalar(af(1536, TC), af(1024, TC), -RC, 127.0,
                                    OP.add, OP.min)
                    last = h.tensor_scalar(hq[:, lo + i, :], af(1536, TC), -128.0,
                                           None, OP.max)
                return last
            E("v", vf, "sv")
        for wh in range(2):
            ic0, ic1 = (0, 11) if wh == 0 else (11, 22)
            W("s", ("st", CNT["st"]), ("sv", CNT["sv"]))
            dvd = 0
            for i in range(ic0, ic1):
                dvd = DMA("s", wdh[:, i - ic0, :], wdq_sp[:, i * P:(i + 1) * P],
                          transpose=True)
            for oc in range(8):
                bk = [0, 1, 4, 5][oc % 4]
                tb([bk])
                W("t", ("da", dvd), ("sv", CNT["sv"]))
                def tf(h, wh=wh, oc=oc, bk=bk, ic0=ic0, ic1=ic1):
                    last = None
                    for i in range(ic0, ic1):
                        last = h.matmul(ps[bk][:, :], wdh[:, i - ic0, oc * P:(oc + 1) * P],
                                        hq[:, i, :], start=(i == ic0), stop=(i == ic1 - 1))
                    return last
                tv = E("t", tf, "st")
                W("v", ("st", tv))
                if wh == 0:
                    vv = E("v", lambda h, oc=oc, bk=bk:
                           h.tensor_copy(af(2048 + oc * TC, TC), ps[bk][:, :]), "sv")
                else:
                    def vf(h, oc=oc, bk=bk, sl=sl):
                        h.tensor_tensor(af(1024, TC), ps[bk][:, :], af(2048 + oc * TC, TC),
                                        OP.add)
                        return h.scalar_tensor_tensor(res[:, oc, sl], af(1024, TC),
                                                      dsc_c[:, oc:oc + 1],
                                                      res[:, oc, sl], OP.mult, OP.add)
                    vv = E("v", vf, "sv")
                guard[bk] = ("sv", vv)
        for tsub in range(4):
            bk = 6 + (tsub % 2)
            tb([bk])
            W("t", ("sv", CNT["sv"]))
            def tf(h, q=q, tsub=tsub, bk=bk):
                last = None
                for dc in range(8):
                    last = h.transpose(pbs[bk - 6][:, dc * P:(dc + 1) * P],
                                       res[:, dc, q * TC + tsub * P:q * TC + (tsub + 1) * P],
                                       eye_b[:, :])
                return last
            tv = E("t", tf, "st")
            W("s", ("st", tv), ("dy", CNT["dy"]))
            sv = E("s", lambda h, bk=bk:
                   h.copy(af(0, 1024), pbs[bk - 6][:, :]), "ss")
            guard[bk] = ("ss", sv)
            W("y", ("ss", sv))
            DMA("y", out_h[q * TC + tsub * P:q * TC + (tsub + 1) * P, :],
                af(0, 1024))
    return finish()


def kernel(**inputs):
    gi = lambda k: np.ascontiguousarray(np.asarray(inputs[k]))
    x = gi("L0_attn_x").reshape(16384, 1024).astype(np.float32)
    sc8 = np.zeros(8, np.float32)
    sc8[0] = np.float32(inputs["L0_ffn_s0"])
    sc8[1] = np.float32(inputs["L0_ffn_s1"])
    sc8[2] = np.float32(inputs["L0_ffn_s2"])
    sc8[3] = np.asarray(inputs["L1_ffn_clip_x"], np.float32).reshape(-1)[0]
    sc8[4] = np.asarray(inputs["L1_ffn_clip_down"], np.float32).reshape(-1)[0]
    shared = dict(
        wq=gi("L0_attn_Wq"), wk=gi("L0_attn_Wk"), wv=gi("L0_attn_Wv"), wo=gi("L0_attn_Wo"),
        bq=gi("L0_attn_bq"), bk=gi("L0_attn_bk"), bv=gi("L0_attn_bv"), bo=gi("L0_attn_bo"),
        k0=gi("L0_ffn_k0").view(np.uint8), k1=gi("L0_ffn_k1").view(np.uint8),
        k2=gi("L0_ffn_k2").view(np.uint8), sc8=sc8,
        win=gi("L1_attn_W_in"), bin=gi("L1_attn_b_in"),
        cw=np.ascontiguousarray(gi("L1_attn_conv_w")[:, 0, :]),
        cb=gi("L1_attn_conv_b"),
        wx=np.ascontiguousarray(gi("L1_attn_W_x")[:, 16:32]),
        bx=np.ascontiguousarray(gi("L1_attn_b_x")[16:32]),
        wdt=gi("L1_attn_W_dt"), bdt=gi("L1_attn_b_dt"),
        alog=gi("L1_attn_A_log"), dvec=gi("L1_attn_D"),
        wout=gi("L1_attn_W_out"), bout=gi("L1_attn_b_out"),
        wg=gi("L1_ffn_w_gate"), wu=gi("L1_ffn_w_up"), wd=gi("L1_ffn_w_down"),
    )
    nc = build(STOP_AFTER, skip=SKIP)
    in_maps = []
    for c in range(8):
        m = dict(shared)
        m["x"] = np.ascontiguousarray(x[c * 2048:(c + 1) * 2048])
        msk = np.zeros(16, np.float32)
        if c > 0:
            msk[c - 1] = 1.0
        msk[8:8 + c] = 1.0
        m["msk"] = msk
        in_maps.append(m)
    res = run_bass_kernel_spmd(nc, in_maps, core_ids=list(range(8)))
    if STOP_AFTER is not None:
        return res
    out = np.concatenate([res.results[c]["y"] for c in range(8)], axis=0)
    return out.reshape(1, 16384, 1024).astype(np.float32)
